# version 12
# baseline (speedup 1.0000x reference)
import contextlib
import math
import numpy as np
import ml_dtypes
import concourse.bass as bass
import concourse.mybir as mybir
from concourse.bass_utils import run_bass_kernel_spmd

F32 = mybir.dt.float32
BF16 = mybir.dt.bfloat16
AF = mybir.ActivationFunctionType
ALU = mybir.AluOpType
AX = mybir.AxisListType
BF = ml_dtypes.bfloat16

B, L, D = 4, 4096, 2048
H, DH, RLO, WIN = 16, 128, 16, 512
DFF, NE = 5632, 8
NCORES = 8
TC, CH, HALO = 2048, 1024, 512
NCH = TC // CH
KV = CH + HALO
NQT, NKT, NIC = CH // 128, KV // 128, D // 128
EPS_LN, EPS_RMS = 1e-5, 1e-6
NEG = -1.0e30
ISD = 1.0 / math.sqrt(DH)
GROUPS = [(0, 1, 2), (3, 4, 5), (6, 7)]
NG = len(GROUPS)
FFC = 256
NFFC = DFF // FFC
NFT = DFF // 128
SCALING = 2.0
LORA_SCALE = 1.0 / 16.0
ENGS = ("tensor", "vector", "scalar", "gpsimd", "sync")
N_OPS_LAST = 0


class Sem:
    __slots__ = ("name", "n", "h", "dma")

    def __init__(self, name, dma=False):
        self.name, self.n, self.h, self.dma = name, 0, None, dma


class Plan:
    def __init__(self):
        self.ops = []
        self.sems = []
        self.mark = {e: {} for e in ENGS}

    def sem(self, name, dma=False):
        s = Sem(name, dma)
        self.sems.append(s)
        return s

    def op(self, eng, fn, waits=(), inc=None, amt=1, is_dma=False, tiny=False):
        assert eng in ENGS, eng
        wl = {}
        for t in waits:
            if t is None:
                continue
            s, v = t
            if s.dma:
                v = s.n
            assert v <= s.n, f"[{eng}] wait {s.name}>={v} but only {s.n} planned"
            if v > self.mark[eng].get(s, 0):
                wl[s] = max(wl.get(s, 0), v)
        for s, v in wl.items():
            self.mark[eng][s] = v
        tok = None
        if inc is not None:
            inc.n += amt
            tok = (inc, inc.n)
            if not is_dma:
                self.mark[eng][inc] = max(self.mark[eng].get(inc, 0), inc.n)
        self.ops.append((eng, fn, tuple(wl.items()), inc, amt, tiny))
        return tok


def _rope_tables():
    half = DH // 2
    theta = 1.0 / 10000.0 ** (np.arange(half, dtype=np.float32) / half)
    fr = np.arange(L, dtype=np.float32)[:, None] * theta[None, :]
    emb = np.concatenate([fr, fr], axis=-1)
    return np.cos(emb).astype(np.float32), np.sin(emb).astype(np.float32)


def _band_mask(d):
    r = np.arange(128)[:, None]
    c = np.arange(128)[None, :]
    diff = d + r - c
    return np.where((diff >= 0) & (diff <= WIN - 1), 0.0, NEG).astype(np.float32)


def _start_masks(first_half):
    out = np.zeros((4, 5, 128, 128), np.float32)
    r = np.arange(128)[:, None]
    c = np.arange(128)[None, :]
    for qt in range(4):
        q0 = 128 * qt
        for kt in range(5):
            j = (q0 - 512 + 128 * kt) + c
            q = q0 + r
            valid = (j <= q) & (j >= q - (WIN - 1))
            if first_half:
                valid = valid & (j >= 0)
            out[qt, kt] = np.where(valid, 0.0, NEG)
    return out


def _prep_core_inputs(core, full):
    b, half = core // 2, core % 2
    t0 = half * TC
    g = lambda k: np.ascontiguousarray(np.asarray(full[k], np.float32))
    x = np.asarray(full["L0_attn_x"])
    if half == 0:
        xs = np.concatenate([np.zeros((HALO, D), np.float32),
                             np.asarray(x[b, 0:TC], np.float32)], axis=0)
    else:
        xs = np.ascontiguousarray(np.asarray(x[b, t0 - HALO:t0 + TC], np.float32))
    cos, sin = _rope_tables()
    pos = np.clip(np.arange(-HALO, TC) + t0, 0, L - 1)
    d = {
        "xs": xs,
        "cosf": np.ascontiguousarray(cos[pos].T.astype(BF)),
        "sinf": np.ascontiguousarray(sin[pos].T.astype(BF)),
        "wqT": np.ascontiguousarray(g("L0_attn_wq").T),
        "wkT": np.ascontiguousarray(g("L0_attn_wk").T),
        "wvT": np.ascontiguousarray(g("L0_attn_wv").T),
        "woT": np.ascontiguousarray(g("L0_attn_wo").T),
        "qAT": np.ascontiguousarray(g("L0_attn_qA").T),
        "kAT": np.ascontiguousarray(g("L0_attn_kA").T),
        "vAT": np.ascontiguousarray(g("L0_attn_vA").T),
        "oAT": np.ascontiguousarray(g("L0_attn_oA").T),
        "qBT": np.ascontiguousarray(g("L0_attn_qB").T),
        "kBT": np.ascontiguousarray(g("L0_attn_kB").T),
        "vBT": np.ascontiguousarray(g("L0_attn_vB").T),
        "oBT": np.ascontiguousarray(g("L0_attn_oB").T),
        "normw": np.ascontiguousarray(g("L0_attn_norm_w").reshape(NIC, 128).T),
        "mask_std": np.stack([_band_mask(WIN), _band_mask(0)]),
        "mask_start": _start_masks(half == 0),
        "identb": np.eye(128, dtype=BF),
        "cvals": np.broadcast_to(np.array(
            [EPS_LN, EPS_RMS, -0.5, -1.0, ISD, 0.0, 0.0, 0.0], np.float32),
            (128, 8)).copy(),
        "identf": np.eye(128, dtype=np.float32),
        "WrT": np.ascontiguousarray(g("L0_ffn_Wr").T),
        "WgT": np.ascontiguousarray(g("L0_ffn_Wg").T),
        "WuT": np.ascontiguousarray(g("L0_ffn_Wu").T),
        "WdT": np.ascontiguousarray(g("L0_ffn_Wd").T),
    }
    Ag, Au = g("L0_ffn_Ag"), g("L0_ffn_Au")
    Bg, Bu = g("L0_ffn_Bg"), g("L0_ffn_Bu")
    Ad, Bd = g("L0_ffn_Ad"), g("L0_ffn_Bd")
    AgS = np.zeros((NG, D, 128), np.float32)
    AuS = np.zeros((NG, D, 128), np.float32)
    BgP = np.zeros((NG, 128, DFF), np.float32)
    BuP = np.zeros((NG, 128, DFF), np.float32)
    Ad8 = np.zeros((NE, DFF, 128), np.float32)
    BdS = np.zeros((NG, 128, D), np.float32)
    REP = np.zeros((NG, 8, 128), np.float32)
    for gi, exps in enumerate(GROUPS):
        for j, e in enumerate(exps):
            AgS[gi, :, 32 * j:32 * j + 16] = Ag[e].T
            AuS[gi, :, 32 * j:32 * j + 16] = Au[e].T
            BgP[gi, 32 * j:32 * j + 16, :] = Bg[e].T
            BuP[gi, 32 * j:32 * j + 16, :] = Bu[e].T
            Ad8[e, :, 32 * j:32 * j + 16] = Ad[e].T
            BdS[gi, 32 * j:32 * j + 16, :] = Bd[e].T
            REP[gi, e, 32 * j:32 * j + 16] = SCALING
    BgE = np.stack([np.ascontiguousarray(Bg[e].T) for e in range(NE)])
    BuE = np.stack([np.ascontiguousarray(Bu[e].T) for e in range(NE)])
    d.update(AgS=AgS, AuS=AuS, BgP=BgP, BuP=BuP, Ad8=Ad8, BdS=BdS, REP=REP,
             BgE=BgE, BuE=BuE)
    return d


def build_program(taps=(), max_ops=None):
    taps = set(taps)
    nc = bass.Bass("TRN2", target_bir_lowering=False)

    di = {}
    def din(name, shape, dtype=F32):
        di[name] = nc.dram_tensor(name, list(shape), dtype, kind="ExternalInput").ap()
    din("xs", (HALO + TC, D))
    din("cosf", (128, HALO + TC), BF16); din("sinf", (128, HALO + TC), BF16)
    for w in ("wqT", "wkT", "wvT", "woT"):
        din(w, (D, D))
    for w in ("qAT", "kAT", "vAT", "oAT"):
        din(w, (D, RLO))
    for w in ("qBT", "kBT", "vBT", "oBT"):
        din(w, (RLO, D))
    din("normw", (128, NIC))
    din("mask_std", (2, 128, 128))
    din("mask_start", (4, 5, 128, 128))
    din("identb", (128, 128), BF16)
    din("cvals", (128, 8))
    din("identf", (128, 128))
    din("WrT", (D, NE))
    din("WgT", (D, DFF)); din("WuT", (D, DFF)); din("WdT", (DFF, D))
    din("AgS", (NG, D, 128)); din("AuS", (NG, D, 128))
    din("BgP", (NG, 128, DFF)); din("BuP", (NG, 128, DFF))
    din("Ad8", (NE, DFF, 128)); din("BdS", (NG, 128, D))
    din("BgE", (NE, RLO, DFF)); din("BuE", (NE, RLO, DFF))
    din("REP", (NG, 8, 128))

    y = nc.dram_tensor("y", [TC, D], F32, kind="ExternalOutput").ap()
    tdram = nc.dram_tensor("tdram", [KV + CH, D], BF16, kind="Internal").ap()
    x2d = nc.dram_tensor("x2d", [CH, D], F32, kind="Internal").ap()
    hbd = nc.dram_tensor("hbd", [DFF, CH], BF16, kind="Internal").ap()
    fd = nc.dram_tensor("fd", [CH, D], F32, kind="Internal").ap()

    tap_aps = {}
    def mktap(name, shape, dtype=F32):
        if name in taps:
            tap_aps[name] = nc.dram_tensor("tap_" + name, list(shape), dtype,
                                           kind="ExternalOutput").ap()
    mktap("h_fm", (128, NIC, KV), BF16)
    mktap("q_h0", (128, CH), BF16)
    mktap("k_h0", (128, KV), BF16)
    mktap("v_h0", (128, NKT, 128), BF16)
    mktap("p_h0", (128, 640), BF16)
    mktap("o_fm", (128, H, CH), BF16)
    mktap("x2", (TC, D))
    mktap("ln3", (128, NIC, CH), BF16)
    mktap("gu0", (128, 2, CH))
    mktap("c3", (128, NG, CH))
    mktap("werm", (8, CH))
    mktap("hb0", (128, 2, CH), BF16)
    def tap(name):
        return tap_aps.get(name)

    P = Plan()
    d = True
    s_x = [P.sem("s_x0", d), P.sem("s_x1", d)]
    s_w = [P.sem("s_w0", d), P.sem("s_w1", d)]
    s_h2 = [P.sem("s_h0", d), P.sem("s_h1", d)]
    s_mi = P.sem("s_mi", d)
    s_wm = P.sem("s_wm", d)
    s_tw = P.sem("s_tw", d)
    sTR = P.sem("sTR", d)
    s_x2d = P.sem("s_x2d", d)
    s_hb = P.sem("s_hb", d)
    s_fd = P.sem("s_fd", d)
    s_tap = P.sem("s_tap", d)
    s_cs = P.sem("s_cs", d)
    s_wd = [P.sem(f"s_wd{i}", d) for i in range(4)]
    s_h3 = [P.sem("s_h30", d), P.sem("s_h31", d)]
    sST = P.sem("sST", d)
    sPE = P.sem("sPE"); sDV = P.sem("sDV"); sAC = P.sem("sAC")
    ALLS = tuple(P.sems)
    sDV2 = P.sem("sDV2"); sAC2 = P.sem("sAC2")
    s_xA = [P.sem("s_xA0", d), P.sem("s_xA1", d)]
    s_twA = P.sem("s_twA", d)

    PBw = [None] * 8
    PBr = [[] for _ in range(8)]

    ctx = contextlib.ExitStack()
    with ctx:
        def sb(name, shape, dtype):
            return ctx.enter_context(nc.sbuf_tensor(name, list(shape), dtype))
        PB = [ctx.enter_context(nc.psum_tensor(f"PB{i}", [128, 512], F32))
              for i in range(8)]

        xt = [sb("xt0", (128, D), F32), sb("xt1", (128, D), F32)]
        sq = sb("sq", (128, D), F32)
        ident_b = sb("ident_b", (128, 128), BF16)
        ident_f = sb("ident_f", (128, 128), F32)
        normw = sb("normw_sb", (128, NIC), F32)
        mstd = sb("mstd", (128, 2, 128), F32)
        m1c = sb("m1c", (128, NQT), F32)
        rs1c = sb("rs1c", (128, NQT), F32)
        rowsum = sb("rowsum", (128, 2), F32)
        invs = sb("invs", (128, 2), F32)
        cA = sb("cA", (128, 16), F32)
        cB = sb("cB", (128, 16), F32)
        cC = sb("cC", (128, 16), F32)
        cD = sb("cD", (128, 16), F32)
        wr_sb = sb("wr_sb", (128, NIC, NE), BF16)
        rep_sb = sb("rep_sb", (8, NG, 128), F32)
        rt_a = sb("rt_a", (128, NE), F32)
        rt_b = sb("rt_b", (128, NE), F32)
        hbuf = [sb("hbuf0", (128, D), BF16), sb("hbuf1", (128, D), BF16)]
        cvals = sb("cvals_sb", (128, 8), F32)
        for _i, _v in enumerate([EPS_LN, EPS_RMS, -0.5, -1.0, ISD]):
            nc.const_aps.register(F32, _v, cvals[:, _i:_i + 1])

        E = P.op

        def mm(fn, waits=(), inc=None, wbanks=(), stop_banks=None):
            w = list(waits) + [PBw[i] for i in wbanks]
            for i in wbanks:
                w += PBr[i]
            tok = E("tensor", fn, waits=w, inc=inc)
            if stop_banks:
                for i in stop_banks:
                    PBw[i] = tok
                    PBr[i] = []
            return tok

        def rd(eng, fn, waits=(), inc=None, banks=(), tiny=False):
            w = list(waits) + [PBw[i] for i in banks]
            tok = E(eng, fn, waits=w, inc=inc, tiny=tiny)
            for i in banks:
                PBr[i].append(tok)
            return tok

        def ld(dst, src, waits=(), sem=None):
            return E("sync", lambda e, dd=dst, s=src: e.dma_start(out=dd, in_=s),
                     waits=waits, inc=sem or s_mi, amt=16, is_dma=True)

        def stb(dst, src, waits=(), sem=None):
            return E("sync", lambda e, dd=dst, s=src: e.dma_start(out=dd, in_=s),
                     waits=waits, inc=sem or s_mi, amt=16, is_dma=True)

        def wcb(dst, src, waits=(), sem=None):
            return E("gpsimd", lambda e, dd=dst, s=src: e.dma_start(out=dd, in_=s),
                     waits=waits, inc=sem or s_wm, amt=16, is_dma=True)

        def trp(dst, src, waits=()):
            return E("sync", lambda e, dd=dst, s=src: e.dma_start_transpose(dd, s),
                     waits=waits, inc=sTR, amt=16, is_dma=True)

        def col(t, i):
            return t[:, i:i + 1]

        def barrier():
            toks = [(s, s.n) for s in ALLS]
            return [E("tensor", lambda e: e.nop(), waits=toks, inc=sPE),
                    E("vector", lambda e: e.engine_nop(), waits=toks, inc=sDV),
                    E("scalar", lambda e: e.nop(), waits=toks, inc=sAC),
                    E("gpsimd", lambda e: e.nop(), waits=toks),
                    E("sync", lambda e: e.nop(), waits=toks)]

        tok_cv = ld(cvals[:], di["cvals"])
        tok_ident = ld(ident_b[:], di["identb"])
        tok_identf = ld(ident_f[:], di["identf"])
        tok_normw = ld(normw[:], di["normw"])
        tok_mstd = ld(mstd[:], di["mask_std"].rearrange("a b c -> b a c"))
        tok_rep = ld(rep_sb[:], di["REP"].rearrange("a b c -> b a c"))
        rr = lambda ap: ap.rearrange("(c p) r -> p c r", p=128)
        tok_wr = wcb(wr_sb[:], rr(di["WrT"]))
        E("scalar", lambda e: e.nop(), waits=[tok_cv], inc=sAC)

        def stats(xtile, wtoks, want_sh, dst_m=None, dst_rs=None,
                  sD=None, sA=None):
            sD = sD or sDV
            sA = sA or sAC
            E("scalar", lambda e: e.activation(sq[:], xtile[:], AF.Identity,
                                               accum_out=col(cA, 0)),
              waits=wtoks, inc=sA, tiny=True)
            tsq = E("scalar", lambda e: e.activation(sq[:], xtile[:], AF.Square,
                                                     accum_out=col(cA, 1)),
                    inc=sA, tiny=True)
            E("vector", lambda e: e.tensor_scalar(cB[:, 0:2], cA[:, 0:2], 1.0 / D,
                                                  None, ALU.mult),
              waits=[tsq], inc=sD, tiny=True)
            E("vector", lambda e: e.tensor_tensor(col(cC, 0), col(cB, 0), col(cB, 0),
                                                  op=ALU.mult), inc=sD, tiny=True)
            tv = E("vector", lambda e: e.tensor_tensor(col(cC, 1), col(cB, 1),
                                                       col(cC, 0), op=ALU.subtract),
                   inc=sD, tiny=True)
            E("scalar", lambda e: e.activation(col(cD, 0), col(cC, 1), AF.Ln,
                                               bias=EPS_LN), waits=[tv], inc=sA,
              tiny=True)
            trs = E("scalar", lambda e: e.activation(col(cD, 1), col(cD, 0), AF.Exp,
                                                     scale=-0.5), inc=sA, tiny=True)
            toks = {"rs": trs}
            if dst_m is not None:
                E("vector", lambda e: e.tensor_copy(dst_m, col(cB, 0)),
                  waits=[trs], inc=sD, tiny=True)
                toks["dst"] = E("vector", lambda e: e.tensor_copy(dst_rs, col(cD, 1)),
                                inc=sD, tiny=True)
            if not want_sh:
                return toks
            E("vector", lambda e: e.tensor_tensor(col(cC, 2), col(cD, 1), col(cD, 1),
                                                  op=ALU.mult), waits=[trs], inc=sD,
              tiny=True)
            t3 = E("vector", lambda e: e.tensor_tensor(col(cC, 3), col(cC, 2),
                                                       col(cC, 1), op=ALU.mult),
                   inc=sD, tiny=True)
            E("scalar", lambda e: e.activation(col(cD, 2), col(cC, 3), AF.Ln,
                                               bias=EPS_RMS), waits=[t3], inc=sA,
              tiny=True)
            tr2 = E("scalar", lambda e: e.activation(col(cD, 3), col(cD, 2), AF.Exp,
                                                     scale=-0.5), inc=sA, tiny=True)
            E("vector", lambda e: e.tensor_tensor(col(cC, 4), col(cD, 1), col(cD, 3),
                                                  op=ALU.mult), waits=[tr2], inc=sD,
              tiny=True)
            toks["sh"] = E("vector", lambda e: e.scalar_tensor_tensor(
                col(cC, 5), col(cB, 0), -1.0, col(cC, 4), ALU.mult, ALU.mult),
                inc=sD, tiny=True)
            return toks

        def emit_A1(chunk, pre):
            r0 = chunk * CH
            hwr_toks = []
            last_hpass = None
            xt_guard = [list(pre), list(pre)]
            for kt in range(NKT):
                xb = xt[kt % 2]
                hx = hbuf[kt % 2]
                tok_x = ld(xb[:], di["xs"][r0 + kt * 128: r0 + (kt + 1) * 128, :],
                           waits=pre + xt_guard[kt % 2], sem=s_xA[kt % 2])
                own = kt - (HALO // 128)
                w = [tok_x] + ([last_hpass] if last_hpass else list(pre))
                if 0 <= own < NQT:
                    tk = stats(xb, w, True, dst_m=col(m1c, own),
                               dst_rs=col(rs1c, own), sD=sDV2, sA=sAC2)
                else:
                    tk = stats(xb, w, True, sD=sDV2, sA=sAC2)
                hbuf_guard = [hwr_toks[kt - 2]] if kt >= 2 else []
                last_hpass = E("scalar", lambda e, o=hx, i=xb:
                               e.activation(o[:], i[:], AF.Identity, bias=col(cC, 5),
                                            scale=col(cC, 4)),
                               waits=[tk["sh"]] + hbuf_guard, inc=sAC2)
                hwr_toks.append(stb(tdram[kt * 128:(kt + 1) * 128, :], hx[:],
                                    waits=[last_hpass], sem=s_twA))
                xt_guard[kt % 2] = [last_hpass]
            return {"hwr": hwr_toks, "xt_guard": xt_guard, "hp": last_hpass}

        def emit_chunk(chunk, a1, prev_opro=None):
            r0 = chunk * CH
            hwr_toks = a1["hwr"]
            actxA = contextlib.ExitStack()
            sbA = lambda n, s, d: actxA.enter_context(nc.sbuf_tensor(f"{n}_c{chunk}", list(s), d))
            actxO = contextlib.ExitStack()
            O_fm = actxO.enter_context(
                nc.sbuf_tensor(f"O_fm_c{chunk}", [128, H, CH], BF16,
                               side="right"))
            h_fm = actxA.enter_context(
                nc.sbuf_tensor(f"h_fm_c{chunk}", [128, NIC, KV], BF16,
                               side="right"))
            cosf = sbA("cosf_sb", (128, KV), BF16)
            sinf = sbA("sinf_sb", (128, KV), BF16)
            mstart = sbA("mstart", (128, 20, 128), F32)

            bar = barrier() if chunk > 0 else []
            if chunk > 0:
                a1toks = [(sDV2, sDV2.n), (sAC2, sAC2.n),
                          (s_xA[0], s_xA[0].n), (s_xA[1], s_xA[1].n),
                          (s_twA, s_twA.n)]
                bar = bar + [E("tensor", lambda e: e.nop(), waits=a1toks, inc=sPE),
                             E("vector", lambda e: e.engine_nop(), waits=a1toks,
                               inc=sDV)]
            tok_cos = ld(cosf[:], di["cosf"][:, r0:r0 + KV], waits=bar, sem=s_cs)
            tok_sin = ld(sinf[:], di["sinf"][:, r0:r0 + KV], sem=s_cs)
            tok_mstart = ld(mstart[:],
                            di["mask_start"].rearrange("a b c d -> c (a b) d"),
                            sem=s_cs)
            hfm_toks = []
            ttr = None
            for ic in range(NIC):
                ttr = trp(h_fm[:, ic, :], tdram[0:KV, ic * 128:(ic + 1) * 128],
                          waits=hwr_toks + ([prev_opro] if prev_opro else []))
            for ic in range(NIC):
                hfm_toks.append(
                    E("vector", lambda e, i=ic: e.tensor_scalar(
                        h_fm[:, i, :], h_fm[:, i, :], col(normw, i), None, ALU.mult),
                      waits=[ttr, tok_normw], inc=sDV))
            a2_tr_val = (sTR, sTR.n)
            if chunk == 0 and tap("h_fm") is not None:
                stb(tap("h_fm"), h_fm[:], waits=hfm_toks, sem=s_tap)
            wqs = [sbA(f"wqs{i}", (128, NIC, 128), BF16) for i in range(2)]
            wks = [sbA(f"wks{i}", (128, NIC, 128), BF16) for i in range(2)]
            wvs = [sbA(f"wvs{i}", (128, NIC, 128), BF16) for i in range(2)]
            qA_sb = sbA("qA_sb", (128, NIC, RLO), BF16)
            kA_sb = sbA("kA_sb", (128, NIC, RLO), BF16)
            vA_sb = sbA("vA_sb", (128, NIC, RLO), BF16)
            qB_sb = sbA("qB_sb", (RLO, D), BF16)
            kB_sb = sbA("kB_sb", (RLO, D), BF16)
            vB_sb = sbA("vB_sb", (RLO, D), BF16)
            a_q = sbA("a_q", (RLO, CH), BF16)
            a_k = sbA("a_k", (RLO, KV), BF16)
            a_v = sbA("a_v", (RLO, KV), BF16)
            K_h = sbA("K_h", (128, KV), BF16)
            V_fmsb = sbA("V_fmsb", (128, KV), BF16)
            V_tm = sbA("V_tm", (128, NKT, 128), BF16)
            Q_h = sbA("Q_h", (128, CH), BF16)
            rop0 = sbA("rop0", (128, 512), F32)
            rop1 = sbA("rop1", (128, 512), F32)
            S_sb2 = sbA("S_sb", (128, 2, 640), F32)
            P_sb2 = sbA("P_sb", (128, 2, 640), BF16)
            PT_sb2 = sbA("PT_sb", (128, 2, 5, 128), BF16)
            tok_qA = wcb(qA_sb[:], rr(di["qAT"]), waits=hwr_toks)
            tok_kA = wcb(kA_sb[:], rr(di["kAT"]))
            tok_vA = wcb(vA_sb[:], rr(di["vAT"]))
            tok_qB = wcb(qB_sb[:], di["qBT"])
            tok_kB = wcb(kB_sb[:], di["kBT"])
            tok_vB = wcb(vB_sb[:], di["vBT"])

            def avec(dst, Asb, n_tok, base, wtok):
                toks = []
                for i, n0 in enumerate(range(0, n_tok, 512)):
                    bk = i % 2
                    pb = PB[bk]
                    for ic in range(NIC):
                        mm(lambda e, j=ic, p=pb, nn=n0, s=(ic == 0),
                           q=(ic == NIC - 1): e.matmul(
                               p[0:16, :], Asb[:, j, :],
                               h_fm[:, j, base + nn: base + nn + 512],
                               start=s, stop=q),
                           waits=([wtok] + hfm_toks) if ic == 0 else [],
                           inc=sPE if ic == NIC - 1 else None,
                           wbanks=(bk,) if ic == 0 else (),
                           stop_banks=(bk,) if ic == NIC - 1 else None)
                    toks.append(rd("vector", lambda e, p=pb, nn=n0: e.tensor_scalar(
                        dst[:, nn:nn + 512], p[0:16, :], LORA_SCALE, None, ALU.mult),
                        inc=sDV, banks=(bk,)))
                return toks
            aq_toks = avec(a_q, qA_sb, CH, HALO, tok_qA)
            ak_toks = avec(a_k, kA_sb, KV, 0, tok_kA)
            av_toks = avec(a_v, vA_sb, KV, 0, tok_vA)

            wtoks = [None, None]
            def head_w(h, guard=()):
                p = h % 2
                sl = lambda w: di[w][:, h * 128:(h + 1) * 128].rearrange(
                    "(c p) m -> p c m", p=128)
                wtoks[p] = [wcb(wqs[p][:], sl("wqT"), waits=guard, sem=s_w[p]),
                            wcb(wks[p][:], sl("wkT"), sem=s_w[p]),
                            wcb(wvs[p][:], sl("wvT"), sem=s_w[p])]
            head_w(0)
            o_evac_toks = []
            prev_rope_guard = None
            prev_attn_pe = None
            for h in range(H):
                p = h % 2
                wt = wtoks[p]
                if h + 1 < H:
                    head_w(h + 1, guard=[prev_attn_pe] if prev_attn_pe else [])

                def proj(wsb, Bsb, av_sb, av_tk, n_tok, base, banks, extra):
                    toks = []
                    for i, n0 in enumerate(range(0, n_tok, 512)):
                        bk = banks[i]
                        pb = PB[bk]
                        for ic in range(NIC):
                            mm(lambda e, j=ic, q=pb, nn=n0: e.matmul(
                                q[:], wsb[:, j, :],
                                h_fm[:, j, base + nn: base + nn + 512],
                                start=(j == 0), stop=False),
                               waits=(wt + extra + hfm_toks + av_tk)
                               if ic == 0 else [],
                               wbanks=(bk,) if ic == 0 else ())
                        toks.append(mm(lambda e, q=pb, nn=n0, hh=h: e.matmul(
                            q[:], Bsb[0:16, hh * 128:(hh + 1) * 128],
                            av_sb[0:16, nn:nn + 512], start=False, stop=True),
                            inc=sPE, stop_banks=(bk,)))
                    return toks
                q_mm = proj(wqs[p], qB_sb, a_q, aq_toks, CH, HALO, (0, 1), [tok_qB])
                k_mm = proj(wks[p], kB_sb, a_k, ak_toks, KV, 0, (2, 3, 4), [tok_kB])
                v_mm = proj(wvs[p], vB_sb, a_v, av_toks, KV, 0, (5, 6, 7), [tok_vB])

                def rope(banks, dst, n_tok, coff, guard, ctab, stab):
                    toks = []
                    for i, n0 in enumerate(range(0, n_tok, 512)):
                        bk = banks[i]
                        pb = PB[bk]
                        cs = slice(coff + n0, coff + n0 + 512)
                        g = [guard] if guard else []
                        if toks:
                            g = [toks[-1]]
                        rd("scalar", lambda e, q=pb: e.activation(
                            rop0[0:64, :], q[64:128, :], AF.Copy, scale=-1.0),
                           waits=g, inc=sAC, banks=(bk,))
                        t2 = rd("scalar", lambda e, q=pb: e.activation(
                            rop0[64:128, :], q[0:64, :], AF.Copy), inc=sAC,
                           banks=(bk,))
                        E("vector", lambda e, c=cs, st_=stab: e.tensor_tensor(
                            rop1[:], rop0[:], st_[:, c], op=ALU.mult),
                          waits=[t2, tok_sin], inc=sDV)
                        t4 = rd("vector", lambda e, q=pb, c=cs, ct=ctab:
                                e.tensor_tensor(
                                    rop0[:], q[:], ct[:, c], op=ALU.mult),
                                waits=[tok_cos], inc=sDV, banks=(bk,))
                        toks.append(E("gpsimd", lambda e, z=slice(n0, n0 + 512),
                                      dd=dst: e.tensor_tensor(
                                          dd[:, z], rop0[:], rop1[:], op=ALU.add),
                                      waits=[prev_attn_pe, t4], inc=sDV))
                    return toks
                tq = rope((0, 1), Q_h, CH, HALO, prev_rope_guard, cosf, sinf)
                tk_r = rope((2, 3, 4), K_h, KV, 0, tq[-1], cosf, sinf)
                prev_rope_guard = tk_r[-1]
                tv = []
                for i, n0 in enumerate(range(0, KV, 512)):
                    bk = (5, 6, 7)[i]
                    tv.append(rd("scalar", lambda e, q=PB[bk], nn=n0:
                                 e.activation(V_fmsb[:, nn:nn + 512], q[:],
                                              AF.Identity),
                                 waits=[prev_attn_pe], inc=sAC, banks=(bk,)))
                vt_toks = []
                for jt in range(NKT):
                    bk = 5 + jt % 2
                    tmm = mm(lambda e, j=jt, q=PB[bk]: e.matmul(
                        q[:, 0:128], V_fmsb[:, j * 128:(j + 1) * 128], ident_b[:],
                        start=True, stop=True),
                        waits=[tv[jt // 4], tok_ident], wbanks=(bk,), inc=sPE,
                        stop_banks=(bk,))
                    vt_toks.append(rd("scalar", lambda e, j=jt, q=PB[bk]:
                                      e.activation(V_tm[:, j, :], q[:, 0:128],
                                                   AF.Identity),
                                      inc=sAC, banks=(bk,)))
                if chunk == 0 and h == 0:
                    if tap("q_h0") is not None:
                        stb(tap("q_h0"), Q_h[:], waits=tq, sem=s_tap)
                    if tap("k_h0") is not None:
                        stb(tap("k_h0"), K_h[:], waits=tk_r, sem=s_tap)
                    if tap("v_h0") is not None:
                        stb(tap("v_h0"), V_tm[:], waits=vt_toks, sem=s_tap)

                def s_mms(qt):
                    pa = qt % 2
                    b0, b1 = (2, 3) if pa == 0 else (0, 1)
                    t1 = mm(lambda e, q=qt, bb=b0: e.matmul(
                        PB[bb][:], Q_h[:, q * 128:(q + 1) * 128],
                        K_h[:, q * 128:q * 128 + 512], start=True, stop=True),
                       waits=tq + tk_r + vt_toks, wbanks=(b0,), inc=sPE,
                       stop_banks=(b0,))
                    t2 = mm(lambda e, q=qt, bb=b1: e.matmul(
                        PB[bb][:, 0:128], Q_h[:, q * 128:(q + 1) * 128],
                        K_h[:, q * 128 + 512:q * 128 + 640], start=True,
                        stop=True),
                       wbanks=(b1,), inc=sPE, stop_banks=(b1,))
                    return (t1, t2)

                def softmax_qt(qt):
                    pa = qt % 2
                    b0, b1 = (2, 3) if pa == 0 else (0, 1)
                    S_sb = S_sb2[:, pa, :]
                    P_sb = P_sb2[:, pa, :]
                    ev = []
                    if chunk == 0 and qt < 4:
                        for k5 in range(4):
                            ev.append(rd("vector", lambda e, k=k5, q=qt, bb=b0,
                                         ss=S_sb: e.scalar_tensor_tensor(
                                             ss[:, k * 128:(k + 1) * 128],
                                             PB[bb][:, k * 128:(k + 1) * 128],
                                             ISD, mstart[:, q * 5 + k, :],
                                             ALU.mult, ALU.add),
                                         waits=[tok_mstart], inc=sDV,
                                         banks=(b0,)))
                        ev.append(rd("vector", lambda e, q=qt, bb=b1, ss=S_sb:
                                     e.scalar_tensor_tensor(
                                         ss[:, 512:640], PB[bb][:, 0:128], ISD,
                                         mstart[:, q * 5 + 4, :],
                                         ALU.mult, ALU.add),
                                     inc=sDV, banks=(b1,)))
                    else:
                        ev.append(rd("vector", lambda e, bb=b0, ss=S_sb:
                                     e.scalar_tensor_tensor(
                            ss[:, 0:128], PB[bb][:, 0:128], ISD, mstd[:, 0, :],
                            ALU.mult, ALU.add),
                            waits=[tok_mstd], inc=sDV, banks=(b0,)))
                        ev.append(rd("vector", lambda e, bb=b0, ss=S_sb:
                                     e.tensor_scalar(
                            ss[:, 128:512], PB[bb][:, 128:512], ISD, None,
                            ALU.mult), inc=sDV, banks=(b0,)))
                        ev.append(rd("vector", lambda e, bb=b1, ss=S_sb:
                                     e.scalar_tensor_tensor(
                            ss[:, 512:640], PB[bb][:, 0:128], ISD, mstd[:, 1, :],
                            ALU.mult, ALU.add), inc=sDV, banks=(b1,)))
                    tbx = E("vector", lambda e, ss=S_sb, p2=pa: e.tensor_reduce(
                        col(cA, 8 + p2), ss[:], axis=AX.X, op=ALU.max,
                        negate=True), waits=ev, inc=sDV, tiny=True)
                    texp = E("scalar", lambda e, ss=S_sb, pp=P_sb, p2=pa:
                             e.activation(pp[:], ss[:], AF.Exp,
                                          bias=col(cA, 8 + p2),
                                          accum_out=col(rowsum, p2)),
                             waits=[tbx], inc=sAC, tiny=True)
                    ti2 = E("vector", lambda e, p2=pa: e.reciprocal(
                        col(invs, p2), col(rowsum, p2)),
                            waits=[texp], inc=sDV, tiny=True)
                    tps = E("vector", lambda e, pp=P_sb, p2=pa: e.tensor_scalar(
                        pp[:], pp[:], col(invs, p2), None, ALU.mult),
                        waits=[ti2], inc=sDV)
                    if chunk == 0 and h == 0 and qt == 4 and tap("p_h0") is not None:
                        tps = stb(tap("p_h0"), P_sb[:], waits=[tps], sem=s_tap)
                    return tps

                def ptpv(qt, tps):
                    nonlocal prev_attn_pe
                    pa = qt % 2
                    P_sb = P_sb2[:, pa, :]
                    PT_sb = PT_sb2[:, pa, :, :]
                    pt_toks = []
                    for k5 in range(5):
                        bk = 4 + (k5 % 2)
                        mm(lambda e, k=k5, q=PB[bk], pp=P_sb: e.matmul(
                            q[:, 0:128], pp[:, k * 128:(k + 1) * 128], ident_b[:],
                            start=True, stop=True),
                           waits=[tps], wbanks=(bk,), inc=sPE, stop_banks=(bk,))
                        pt_toks.append(rd("scalar", lambda e, k=k5, q=PB[bk],
                                          pt_=PT_sb: e.activation(
                                              pt_[:, k, :], q[:, 0:128],
                                              AF.Identity),
                                          inc=sAC, banks=(bk,)))
                    o_mm = None
                    for k5 in range(5):
                        o_mm = mm(lambda e, k=k5, q=qt, pt_=PT_sb: e.matmul(
                            PB[6][:, 0:128], V_tm[:, q + k, :], pt_[:, k, :],
                            start=(k == 0), stop=(k == 4)),
                            waits=[pt_toks[k5]], wbanks=(6,) if k5 == 0 else (),
                            inc=sPE if k5 == 4 else None,
                            stop_banks=(6,) if k5 == 4 else None)
                    prev_attn_pe = o_mm
                    o_evac_toks.append(rd("scalar", lambda e, hh=h, q=qt:
                                          e.activation(
                                              O_fm[:, hh, q * 128:(q + 1) * 128],
                                              PB[6][:, 0:128], AF.Identity),
                                          inc=sAC, banks=(6,)))

                s_mms(0)
                prev_tps = None
                for qt in range(1, NQT):
                    s_mms(qt)
                    prev_tps = softmax_qt(qt - 1)
                    ptpv(qt - 1, prev_tps)
                prev_tps = softmax_qt(NQT - 1)
                ptpv(NQT - 1, prev_tps)
            if chunk == 0 and tap("o_fm") is not None:
                stb(tap("o_fm"), O_fm[:], waits=o_evac_toks, sem=s_tap)

            actxA.close()
            ctxC = contextlib.ExitStack()
            c3 = [ctxC.enter_context(nc.sbuf_tensor(f"c3_{g}_c{chunk}",
                                                    [128, CH], F32))
                  for g in range(NG)]
            we_rm = ctxC.enter_context(
                nc.sbuf_tensor(f"we_rm_c{chunk}", [8, CH], F32))
            ctxL = contextlib.ExitStack()
            ln3_fm = ctxL.enter_context(
                nc.sbuf_tensor(f"ln3_fm_c{chunk}", [128, NIC, CH], BF16))
            actx4 = contextlib.ExitStack()
            sb4 = lambda n, s, d: actx4.enter_context(nc.sbuf_tensor(f"{n}_c{chunk}", list(s), d))
            wo_full = sb4("wo_full", (128, NIC, D), BF16)
            a_o = sb4("a_o", (RLO, CH), BF16)
            ln1_sb = sb4("ln1_sb", (128, D), F32)
            a_ep = sb4("a_ep", (128, D), F32)
            ln3tmb = sb4("ln3tmb", (128, D), BF16)
            oA_sb = sb4("oA_sb", (128, NIC, RLO), BF16)
            oB_sb = sb4("oB_sb", (RLO, D), BF16)

            bar4 = barrier()
            wo_toks = [wcb(wo_full[:, :, d0 * 512:(d0 + 1) * 512],
                           di["woT"][:, d0 * 512:(d0 + 1) * 512]
                           .rearrange("(c p) m -> p c m", p=128),
                           waits=bar4 if d0 == 0 else (), sem=s_wd[d0])
                       for d0 in range(4)]
            tok_wo = wo_toks[-1]
            tok_oA = wcb(oA_sb[:], rr(di["oAT"]))
            tok_oB = wcb(oB_sb[:], di["oBT"])
            ao_toks = []
            for n0 in range(0, CH, 512):
                for ic in range(NIC):
                    mm(lambda e, j=ic, nn=n0, s=(ic == 0), q=(ic == NIC - 1):
                       e.matmul(PB[7][0:16, :], oA_sb[:, j, :],
                                O_fm[:, j, nn:nn + 512], start=s, stop=q),
                       waits=([tok_oA] + bar4 + o_evac_toks) if ic == 0 else [],
                       wbanks=(7,) if ic == 0 else (),
                       inc=sPE if ic == NIC - 1 else None,
                       stop_banks=(7,) if ic == NIC - 1 else None)
                ao_toks.append(rd("vector", lambda e, nn=n0: e.tensor_scalar(
                    a_o[:, nn:nn + 512], PB[7][0:16, :], LORA_SCALE, None, ALU.mult),
                    inc=sDV, banks=(7,)))

            x2_toks = []
            l3w_toks = []
            ep_xt_guard = [list(bar4), list(bar4)]
            last_ep = [None, None]
            last_opro = [None]
            prev_tx2 = [None, None]
            for qt in range(NQT):
                bs = 0 if qt % 2 == 0 else 4
                op_toks = []
                for dc in range(4):
                    bk = bs + dc
                    pb = PB[bk]
                    last_opro[0] = None
                    for ic in range(NIC):
                        mm(lambda e, j=ic, q=qt, d=dc, p=pb: e.matmul(
                            p[:], O_fm[:, j, q * 128:(q + 1) * 128],
                            wo_full[:, j, d * 512:(d + 1) * 512],
                            start=(j == 0), stop=False),
                           waits=([wo_toks[dc], tok_oB] + ao_toks)
                           if ic == 0 else [],
                           wbanks=(bk,) if ic == 0 else ())
                    op_toks.append(mm(lambda e, q=qt, d=dc, p=pb: e.matmul(
                        p[:], a_o[0:16, q * 128:(q + 1) * 128],
                        oB_sb[0:16, d * 512:(d + 1) * 512], start=False, stop=True),
                        inc=sPE, stop_banks=(bk,)))
                    last_opro[0] = op_toks[-1]
                xb = xt[qt % 2]
                tok_x = ld(xb[:], di["xs"][r0 + HALO + qt * 128:
                                           r0 + HALO + (qt + 1) * 128, :],
                           waits=bar4 + ep_xt_guard[qt % 2], sem=s_x[qt % 2])
                tb1 = E("vector", lambda e, q=qt: e.scalar_tensor_tensor(
                    col(cC, 8), col(m1c, q), -1.0, col(rs1c, q), ALU.mult, ALU.mult),
                    inc=sDV, tiny=True)
                ln1_guard = [x2_toks[qt - 1]] if qt >= 1 else []
                tln1 = E("scalar", lambda e, q=qt, x=xb: e.activation(
                    ln1_sb[:], x[:], AF.Identity, bias=col(cC, 8),
                    scale=col(rs1c, q)),
                    waits=[tok_x, tb1] + ln1_guard, inc=sAC)
                ta = []
                for dc in range(4):
                    bk = bs + dc
                    ta.append(rd("vector", lambda e, d=dc, p=PB[bs + dc]:
                                 e.scalar_tensor_tensor(
                                     a_ep[:, d * 512:(d + 1) * 512], p[:], 1.0,
                                     ln1_sb[:, d * 512:(d + 1) * 512], ALU.mult,
                                     ALU.add, accum_out=col(cA, d)),
                                 waits=[tln1, op_toks[dc]], inc=sDV, banks=(bk,),
                                 tiny=True))
                tcm1 = E("vector", lambda e: e.tensor_tensor(
                    col(cA, 4), col(cA, 0), col(cA, 1), op=ALU.add),
                    waits=ta, inc=sDV, tiny=True)
                E("vector", lambda e: e.tensor_tensor(col(cA, 5), col(cA, 2),
                                                      col(cA, 3), op=ALU.add),
                  inc=sDV, tiny=True)
                E("vector", lambda e: e.tensor_tensor(col(cA, 0), col(cA, 4),
                                                      col(cA, 5), op=ALU.add),
                  inc=sDV, tiny=True)
                tsqa = E("scalar", lambda e: e.activation(
                    sq[:], a_ep[:], AF.Square, accum_out=col(cA, 1)),
                    waits=[tcm1], inc=sAC, tiny=True)
                E("vector", lambda e: e.tensor_scalar(cB[:, 0:2], cA[:, 0:2],
                                                      1.0 / D, None, ALU.mult),
                  waits=[tsqa], inc=sDV, tiny=True)
                E("vector", lambda e: e.tensor_tensor(col(cC, 0), col(cB, 0),
                                                      col(cB, 0), op=ALU.mult),
                  inc=sDV, tiny=True)
                tv2 = E("vector", lambda e: e.tensor_tensor(
                    col(cC, 1), col(cB, 1), col(cC, 0), op=ALU.subtract), inc=sDV,
                    tiny=True)
                E("scalar", lambda e: e.activation(col(cD, 0), col(cC, 1), AF.Ln,
                                                   bias=EPS_LN), waits=[tv2],
                  inc=sAC, tiny=True)
                trs2 = E("scalar", lambda e: e.activation(col(cD, 1), col(cD, 0),
                                                          AF.Exp, scale=-0.5),
                         inc=sAC, tiny=True)
                tx2a = E("vector", lambda e: e.tensor_scalar(
                    sq[:], a_ep[:], col(cB, 0), col(cD, 1), ALU.subtract, ALU.mult),
                    waits=[trs2], inc=sDV)
                tx2 = E("vector", lambda e, x=xb: e.tensor_tensor(
                    ln1_sb[:], sq[:], x[:], op=ALU.add), inc=sDV)
                ep_xt_guard[qt % 2] = [tx2]
                last_ep[0] = tx2
                x2_toks.append(stb(x2d[qt * 128:(qt + 1) * 128, :],
                                   ln1_sb[:], waits=[tx2], sem=s_x2d))
                tsq3 = E("vector", lambda e: e.scalar_tensor_tensor(
                    sq[:], ln1_sb[:], 1.0, ln1_sb[:], ALU.mult, ALU.mult,
                    accum_out=col(cA, 7)), waits=[tx2], inc=sDV, tiny=True)
                E("vector", lambda e: e.tensor_scalar(col(cB, 7), col(cA, 7),
                                                      1.0 / D, None, ALU.mult),
                  inc=sDV, tiny=True)
                E("vector", lambda e, q=qt: e.tensor_tensor(
                    col(cC, 0), col(m1c, q), col(m1c, q), op=ALU.mult),
                  inc=sDV, tiny=True)
                tv3 = E("vector", lambda e: e.tensor_tensor(
                    col(cC, 1), col(cB, 7), col(cC, 0), op=ALU.subtract), inc=sDV,
                    tiny=True)
                E("scalar", lambda e: e.activation(col(cD, 2), col(cC, 1), AF.Ln,
                                                   bias=EPS_LN), waits=[tv3],
                  inc=sAC, tiny=True)
                trs3 = E("scalar", lambda e: e.activation(col(cD, 3), col(cD, 2),
                                                          AF.Exp, scale=-0.5),
                         inc=sAC, tiny=True)
                tb3 = E("vector", lambda e, q=qt: e.scalar_tensor_tensor(
                    col(cC, 3), col(m1c, q), -1.0, col(cD, 3), ALU.mult,
                    ALU.mult), waits=[trs3], inc=sDV, tiny=True)
                tc7 = E("vector", lambda e: e.tensor_copy(col(cC, 7), col(cD, 3)),
                        inc=sDV, tiny=True)
                l3_guard = [l3w_toks[qt - 1]] if qt >= 1 else []
                tl3 = E("scalar", lambda e: e.activation(
                    ln3tmb[:], ln1_sb[:], AF.Identity, bias=col(cC, 3),
                    scale=col(cC, 7)), waits=[tb3, tc7] + l3_guard, inc=sAC)
                l3w_toks.append(stb(tdram[KV + qt * 128:KV + (qt + 1) * 128, :],
                                    ln3tmb[:], waits=[tl3], sem=s_tw))
                last_ep[1] = tl3
            l3_toks = []
            for ic in range(NIC):
                l3_toks.append(trp(ln3_fm[:, ic, :],
                                   tdram[KV:KV + CH, ic * 128:(ic + 1) * 128],
                                   waits=l3w_toks))
            if chunk == 0 and tap("ln3") is not None:
                stb(tap("ln3"), ln3_fm[:], waits=l3_toks, sem=s_tap)
            if chunk == 0 and tap("x2") is not None:
                stb(tap("x2")[0:CH, :], x2d, waits=x2_toks, sem=s_tap)
            a1_next = None
            if chunk + 1 < NCH:
                a1_next = emit_A1(chunk + 1,
                                  pre=[last_ep[0], last_ep[1], a2_tr_val])

            actx4.close()
            actxO.close()
            actxB = contextlib.ExitStack()
            sbB = lambda n, s, d: actxB.enter_context(nc.sbuf_tensor(f"{n}_c{chunk}", list(s), d))
            wg_c = [sbB(f"wg_c{i}", (128, NIC, FFC), BF16) for i in range(2)]
            wu_c = [sbB(f"wu_c{i}", (128, NIC, FFC), BF16) for i in range(2)]
            bg_c = [sbB(f"bg_c{i}", (128, NE, FFC), BF16) for i in range(2)]
            bu_c = [sbB(f"bu_c{i}", (128, NE, FFC), BF16) for i in range(2)]
            ad_c = [sbB(f"ad_c{i}", (128, 2, NE, 128), BF16) for i in range(2)]
            ags_sb = sbB("ags_sb", (128, NIC, 128), BF16)
            aus_sb = sbB("aus_sb", (128, NIC, 128), BF16)
            g_sb = sbB("g_sb", (128, 2, CH), BF16)
            u_sb = sbB("u_sb", (128, 2, CH), BF16)
            silu_sb = sbB("silu_sb", (128, CH), F32)
            he_sb = sbB("he_sb", (128, CH), BF16)
            hbt = [sbB(f"hbt{i}", (128, CH), BF16) for i in range(2)]
            a_g3 = [sbB(f"a_g3_{g}", (128, CH), BF16) for g in range(NG)]
            a_u3 = [sbB(f"a_u3_{g}", (128, CH), BF16) for g in range(NG)]

            barB = barrier()
            c3z = [E("vector", lambda e, g2=g2: e.memset(c3[g2][:], 0.0),
                     waits=barB, inc=sDV) for g2 in range(NG)]
            mz_bg = [E("vector", lambda e, t=bg_c[i]: e.memset(t[:], 0.0),
                       waits=barB, inc=sDV) for i in range(2)]
            mz_bu = [E("vector", lambda e, t=bu_c[i]: e.memset(t[:], 0.0),
                       waits=barB, inc=sDV) for i in range(2)]

            werm_toks = []
            for qt in range(NQT):
                for ic in range(NIC):
                    mm(lambda e, j=ic, q=qt, s=(ic == 0), z=(ic == NIC - 1):
                       e.matmul(PB[7][:, 0:NE], ln3_fm[:, j, q * 128:(q + 1) * 128],
                                wr_sb[:, j, :], start=s, stop=z),
                       waits=([tok_wr] + l3_toks + barB) if ic == 0 else [],
                       wbanks=(7,) if ic == 0 else (),
                       inc=sPE if ic == NIC - 1 else None,
                       stop_banks=(7,) if ic == NIC - 1 else None)
                rd("vector", lambda e: e.tensor_copy(rt_a[:], PB[7][:, 0:NE]),
                   inc=sDV, banks=(7,), tiny=True)
                tmx = E("vector", lambda e: e.tensor_reduce(
                    col(cA, 9), rt_a[:], axis=AX.X, op=ALU.max, negate=True),
                    inc=sDV, tiny=True)
                tex = E("scalar", lambda e: e.activation(
                    rt_b[:], rt_a[:], AF.Exp, bias=col(cA, 9),
                    accum_out=col(cA, 10)), waits=[tmx], inc=sAC, tiny=True)
                E("scalar", lambda e: e.activation(col(cD, 9), col(cA, 10), AF.Ln),
                  inc=sAC, tiny=True)
                tiv = E("scalar", lambda e: e.activation(col(cD, 10), col(cD, 9),
                                                         AF.Exp, scale=-1.0),
                        inc=sAC, tiny=True)
                tk1 = E("vector", lambda e: e.tensor_scalar(
                    rt_a[:], rt_b[:], 1.0, None, ALU.is_ge), waits=[tex], inc=sDV,
                    tiny=True)
                E("vector", lambda e: e.tensor_tensor(rt_a[:], rt_b[:], rt_a[:],
                                                      op=ALU.mult), inc=sDV,
                  tiny=True)
                E("vector", lambda e: e.tensor_tensor(rt_a[:], rt_b[:], rt_a[:],
                                                      op=ALU.subtract), inc=sDV,
                  tiny=True)
                E("vector", lambda e: e.tensor_reduce(
                    col(cA, 11), rt_a[:], axis=AX.X, op=ALU.max), inc=sDV,
                  tiny=True)
                E("vector", lambda e: e.tensor_copy(col(cA, 12), col(cA, 11)),
                  inc=sDV, tiny=True)
                tkp = E("vector", lambda e: e.tensor_scalar(
                    rt_a[:], rt_b[:], col(cA, 11), None, ALU.is_ge), inc=sDV,
                    tiny=True)
                E("vector", lambda e: e.tensor_tensor(rt_b[:], rt_b[:], rt_a[:],
                                                      op=ALU.mult), inc=sDV,
                  tiny=True)
                twe = E("vector", lambda e: e.tensor_scalar(
                    rt_b[:], rt_b[:], col(cD, 10), None, ALU.mult),
                    waits=[tiv], inc=sDV, tiny=True)
                twt = mm(lambda e: e.matmul(PB[6][0:NE, 0:128], rt_b[:],
                                            ident_f[:], start=True, stop=True),
                         waits=[twe, tok_identf], wbanks=(6,), inc=sPE,
                         stop_banks=(6,))
                werm_toks.append(rd("vector", lambda e, q=qt: e.tensor_copy(
                    we_rm[:, q * 128:(q + 1) * 128], PB[6][0:NE, 0:128]),
                    inc=sDV, banks=(6,), tiny=True))

            def avec3(dstl, Asb, srcs):
                out = []
                prev_mm = None
                for g2 in range(NG):
                    tok_w = wcb(Asb[:], rr(srcs[g2]),
                                waits=[prev_mm] if prev_mm else barB)
                    for i, n0 in enumerate(range(0, CH, 512)):
                        bk = i % 2
                        pb = PB[bk]
                        for ic in range(NIC):
                            prev_mm = mm(lambda e, j=ic, p=pb, nn=n0,
                                         s=(ic == 0), z=(ic == NIC - 1):
                                         e.matmul(p[:], Asb[:, j, :],
                                                  ln3_fm[:, j, nn:nn + 512],
                                                  start=s, stop=z),
                                         waits=([tok_w] + l3_toks)
                                         if ic == 0 else [],
                                         wbanks=(bk,) if ic == 0 else (),
                                         inc=sPE if ic == NIC - 1 else None,
                                         stop_banks=(bk,)
                                         if ic == NIC - 1 else None)
                        out.append(rd("vector", lambda e, p=pb, nn=n0, g3=g2:
                                      e.tensor_scalar(dstl[g3][:, nn:nn + 512],
                                                      p[:], SCALING, None,
                                                      ALU.mult),
                                      inc=sDV, banks=(bk,)))
                return out
            ag_toks = avec3(a_g3, ags_sb, [di["AgS"][g] for g in range(NG)])
            au_toks = avec3(a_u3, aus_sb, [di["AuS"][g] for g in range(NG)])

            wB = [None, None]
            def ffw(fi, waits=()):
                p = fi % 2
                f0 = fi * FFC
                wB[p] = [
                    wcb(wg_c[p][:], di["WgT"][:, f0:f0 + FFC].rearrange(
                        "(c p) m -> p c m", p=128), waits=waits, sem=s_w[p]),
                    wcb(wu_c[p][:], di["WuT"][:, f0:f0 + FFC].rearrange(
                        "(c p) m -> p c m", p=128), sem=s_w[p]),
                    wcb(ad_c[p][:, 0, :, :], di["Ad8"][:, f0:f0 + 128, :]
                        .rearrange("e p m -> p e m"), sem=s_w[p]),
                    wcb(ad_c[p][:, 1, :, :], di["Ad8"][:, f0 + 128:f0 + 256, :]
                        .rearrange("e p m -> p e m"), sem=s_w[p]),
                ]
                for g2x, exps in enumerate(GROUPS):
                    for jx, e_idx in enumerate(exps):
                        wB[p].append(wcb(
                            bg_c[p][32 * jx:32 * jx + 16, e_idx, :],
                            di["BgE"][e_idx, :, f0:f0 + FFC],
                            waits=[mz_bg[p]] if fi < 2 else [], sem=s_w[p]))
                        wB[p].append(wcb(
                            bu_c[p][32 * jx:32 * jx + 16, e_idx, :],
                            di["BuE"][e_idx, :, f0:f0 + FFC],
                            waits=[mz_bu[p]] if fi < 2 else [], sem=s_w[p]))
            ffw(0, waits=barB)
            hb_w_toks = []
            prev_he_pe = None
            prev_hbt = [None, None]
            for fi in range(NFFC):
                p = fi % 2
                wt = wB[p]
                if fi + 1 < NFFC:
                    ffw(fi + 1, waits=[prev_he_pe] if prev_he_pe else [])
                guev = [[None, None], [None, None]]
                hb_last = [None]
                for pt in range(2):
                    fc = 2 * fi + pt
                    for ic in range(NIC):
                        for nb, n0 in enumerate((0, 512)):
                            mm(lambda e, j=ic, q=PB[nb], nn=n0, w=wg_c[p], b=pt:
                               e.matmul(
                                q[:], w[:, j, b * 128:(b + 1) * 128],
                                ln3_fm[:, j, nn:nn + 512],
                                start=(j == 0), stop=(j == NIC - 1)),
                               waits=(wt + l3_toks) if ic == 0 else [],
                               wbanks=(nb,) if ic == 0 else (),
                               inc=sPE if ic == NIC - 1 else None,
                               stop_banks=(nb,) if ic == NIC - 1 else None)
                    for ic in range(NIC):
                        for nb, n0 in enumerate((0, 512)):
                            mm(lambda e, j=ic, q=PB[2 + nb], nn=n0, w=wu_c[p],
                               b=pt: e.matmul(
                                q[:], w[:, j, b * 128:(b + 1) * 128],
                                ln3_fm[:, j, nn:nn + 512],
                                start=(j == 0), stop=(j == NIC - 1)),
                               wbanks=(2 + nb,) if ic == 0 else (),
                               inc=sPE if ic == NIC - 1 else None,
                               stop_banks=(2 + nb,) if ic == NIC - 1 else None)
                    hb_done = []
                    for nb, n0 in enumerate((0, 512)):
                        tsl0 = rd("scalar", lambda e, q=PB[nb], nn=n0:
                                  e.activation(silu_sb[:, nn:nn + 512], q[:],
                                               AF.Silu),
                                  waits=[prev_he_pe], inc=sAC, banks=(nb,))
                        guev[pt][0] = rd(
                            "scalar", lambda e, q=PB[nb], nn=n0, b=pt:
                            e.activation(g_sb[:, b, nn:nn + 512], q[:],
                                         AF.Identity),
                            waits=[prev_he_pe], inc=sAC, banks=(nb,))
                        guev[pt][1] = rd(
                            "vector", lambda e, q=PB[2 + nb], nn=n0, b=pt:
                            e.tensor_copy(u_sb[:, b, nn:nn + 512], q[:]),
                            waits=[prev_he_pe], inc=sDV, banks=(2 + nb,))
                        hb_done.append(E("vector", lambda e, nn=n0, b=pt:
                                         e.tensor_tensor(
                                             hbt[b][:, nn:nn + 512],
                                             silu_sb[:, nn:nn + 512],
                                             u_sb[:, b, nn:nn + 512],
                                             op=ALU.mult),
                                         waits=[prev_hbt[pt], tsl0], inc=sDV))
                    tok_hw = stb(hbd[fc * 128:(fc + 1) * 128, :], hbt[pt][:],
                                 waits=hb_done, sem=s_hb)
                    prev_hbt[pt] = tok_hw
                    hb_w_toks.append(tok_hw)
                    hb_last[0] = hb_done[-1]
                slots = [(g2, j, e_id, pt, nb)
                         for g2, exps in enumerate(GROUPS)
                         for j, e_id in enumerate(exps)
                         for pt in range(2)
                         for nb in range(2)]
                ns = len(slots)
                pend = [None] * ns
                the_toks = [None] * ns
                ad_toks = [None] * ns
                glb = lambda i: 4 + 2 * (i % 2)
                ulb = lambda i: 5 + 2 * (i % 2)

                def emit_slot_mms(i):
                    g2, j, e_id, pt, nb = slots[i]
                    n0 = nb * 512
                    mm(lambda e, q=PB[glb(i)], nn=n0, b=pt: e.matmul(
                        q[:], ident_b[:], g_sb[:, b, nn:nn + 512],
                        start=True, stop=False),
                       waits=((wt + ag_toks + au_toks) if i == 0 else [])
                       + guev[pt], wbanks=(glb(i),))
                    bg_tok = mm(lambda e, q=PB[glb(i)], nn=n0, ee=e_id, g3=g2,
                                w=bg_c[p], b=pt: e.matmul(
                                    q[:], w[:, ee, b * 128:(b + 1) * 128],
                                    a_g3[g3][:, nn:nn + 512],
                                    start=False, stop=True),
                                inc=sPE, stop_banks=(glb(i),))
                    mm(lambda e, q=PB[ulb(i)], nn=n0, b=pt: e.matmul(
                        q[:], ident_b[:], u_sb[:, b, nn:nn + 512],
                        start=True, stop=False),
                       waits=guev[pt], wbanks=(ulb(i),))
                    bu_tok = mm(lambda e, q=PB[ulb(i)], nn=n0, ee=e_id, g3=g2,
                                w=bu_c[p], b=pt: e.matmul(
                                    q[:], w[:, ee, b * 128:(b + 1) * 128],
                                    a_u3[g3][:, nn:nn + 512],
                                    start=False, stop=True),
                                inc=sPE, stop_banks=(ulb(i),))
                    pend[i] = (bg_tok, bu_tok)

                def emit_slot_ew(i):
                    g2, j, e_id, pt, nb = slots[i]
                    n0 = nb * 512
                    tsl = rd("scalar", lambda e, q=PB[glb(i)], nn=n0:
                             e.activation(silu_sb[:, nn:nn + 512], q[:],
                                          AF.Silu),
                             waits=[pend[i][0]]
                             + ([hb_last[0]] if i < 2 else [the_toks[i - 2]]),
                             inc=sAC, banks=(glb(i),))
                    the_toks[i] = rd(
                        "vector", lambda e, q=PB[ulb(i)], nn=n0:
                        e.tensor_tensor(
                            he_sb[:, nn:nn + 512], silu_sb[:, nn:nn + 512],
                            q[:], op=ALU.mult),
                        waits=[tsl] + ([ad_toks[i - 2]] if i >= 2 else []),
                        inc=sDV, banks=(ulb(i),))

                def emit_slot_ad(i):
                    nonlocal prev_he_pe
                    g2, j, e_id, pt, nb = slots[i]
                    n0 = nb * 512
                    first = (j == 0 and pt == 0)
                    last = (j == len(GROUPS[g2]) - 1 and pt == 1)
                    ad_tok = mm(lambda e, q=PB[nb], nn=n0, ee=e_id, b=pt,
                                s=first, z=last, w=ad_c[p]: e.matmul(
                                    q[:], w[:, b, ee, :], he_sb[:, nn:nn + 512],
                                    start=s, stop=z),
                                waits=[the_toks[i]],
                                wbanks=(nb,) if first else (),
                                inc=sPE,
                                stop_banks=(nb,) if last else None)
                    ad_toks[i] = ad_tok
                    prev_he_pe = ad_tok
                    if last:
                        rd("vector", lambda e, q=PB[nb], nn=n0, g3=g2:
                           e.tensor_tensor(c3[g3][:, nn:nn + 512], q[:],
                                           c3[g3][:, nn:nn + 512], op=ALU.add),
                           waits=c3z, inc=sDV, banks=(nb,))

                emit_slot_mms(0)
                emit_slot_mms(1)
                emit_slot_ew(0)
                for i in range(2, ns):
                    emit_slot_mms(i)
                    emit_slot_ew(i - 1)
                    emit_slot_ad(i - 2)
                emit_slot_ew(ns - 1)
                emit_slot_ad(ns - 2)
                emit_slot_ad(ns - 1)
            if chunk == 0 and tap("c3") is not None:
                for g2 in range(NG):
                    stb(tap("c3")[:, g2, :], c3[g2][:], sem=s_tap)
            if chunk == 0 and tap("werm") is not None:
                stb(tap("werm"), we_rm[:], waits=werm_toks, sem=s_tap)

            actxB.close()
            ctxL.close()
            actxB2 = contextlib.ExitStack()
            sbB2 = lambda n, s, d: actxB2.enter_context(
                nc.sbuf_tensor(f"{n}_c{chunk}", list(s), d))
            wd_dc = sbB2("wd_dc", (128, NFT, 1024), BF16)
            h_t = [sbB2(f"h_t{i}", (128, NFT, 128), BF16) for i in range(2)]
            fst = [sbB2(f"fst{i}", (128, 1024), F32) for i in range(2)]
            c3b = [sbB2(f"c3b_{g}", (128, CH), BF16) for g in range(NG)]
            bds_sb = sbB2("bds_sb", (128, NG, 1024), BF16)
            fbB = sbB2("fbB", (128, D), F32)
            x2bB = sbB2("x2bB", (128, D), F32)
            sqB = sbB2("sqB", (128, D), F32)
            cB4 = sbB2("cB4", (128, 8), F32)

            barB2 = barrier()
            c3b_toks = []
            for g2 in range(NG):
                for nb, n0 in enumerate((0, 512)):
                    tw = mm(lambda e, q=PB[4 + nb], nn=n0, g3=g2: e.matmul(
                        q[:], rep_sb[:, g3, :], we_rm[0:NE, nn:nn + 512],
                        start=True, stop=True),
                        waits=[tok_rep] + werm_toks + barB2, wbanks=(4 + nb,),
                        inc=sPE, stop_banks=(4 + nb,))
                    c3b_toks.append(rd("vector", lambda e, q=PB[4 + nb], nn=n0,
                                       g3=g2: e.tensor_tensor(
                                           c3b[g3][:, nn:nn + 512],
                                           c3[g3][:, nn:nn + 512], q[:],
                                           op=ALU.mult), inc=sDV,
                                       banks=(4 + nb,)))
            fd_w = [[] for _ in range(NQT)]
            prev_ht_mm = [None, None]
            prev_fst = [None, None]
            def wd_load(dcp, fg, waits=()):
                f0, f1 = fg * 11, (fg + 1) * 11
                return wcb(wd_dc[:, f0:f1, :],
                           di["WdT"][f0 * 128:f1 * 128,
                                     dcp * 1024:(dcp + 1) * 1024]
                           .rearrange("(c p) m -> p c m", p=128),
                           waits=waits, sem=s_wd[fg])
            def bds_load(dcp, waits=()):
                return wcb(bds_sb[:],
                           di["BdS"][:, :, dcp * 1024:(dcp + 1) * 1024]
                           .rearrange("a b c -> b a c"), waits=waits)
            bds_toks = [bds_load(0, waits=barB2), None]
            wd_toks2 = [[wd_load(0, fg, waits=barB2) for fg in range(4)], None]
            fg_last = [None] * 4
            for dcp in range(2):
                wd_toks = wd_toks2[dcp]
                tok_bds = bds_toks[dcp]
                for t in range(NQT):
                    bp = t % 2
                    tok_h = ld(h_t[bp][:, 0:22, :],
                               hbd[0:22 * 128, t * 128:(t + 1) * 128]
                               .rearrange("(c p) m -> p c m", p=128),
                               waits=hb_w_toks + [prev_ht_mm[bp]], sem=s_h2[bp])
                    tok_h2 = E("gpsimd", lambda e, tt=t, b=bp: e.dma_start(
                        out=h_t[b][:, 22:NFT, :],
                        in_=hbd[22 * 128:NFT * 128, tt * 128:(tt + 1) * 128]
                        .rearrange("(c p) m -> p c m", p=128)),
                        waits=hb_w_toks + [prev_ht_mm[bp]], inc=s_h3[bp],
                        amt=16, is_dma=True)
                    dmm = None
                    for nb in range(2):
                        bk = 2 * bp + nb
                        dc = 2 * dcp + nb
                        for fc in range(NFT):
                            w = [tok_h] if fc == 0 else []
                            if fc == 22 and nb == 0:
                                w.append(tok_h2)
                            if fc % 11 == 0:
                                w.append(wd_toks[fc // 11])
                            lastrd = (dcp == 0 and t == NQT - 1 and nb == 1
                                      and fc % 11 == 10)
                            mtk = mm(lambda e, f=fc, q=PB[bk], ht=h_t[bp],
                                     b2=nb: e.matmul(
                                q[:], ht[:, f, :],
                                wd_dc[:, f, b2 * 512:(b2 + 1) * 512],
                                start=(f == 0), stop=False),
                               waits=w, wbanks=(bk,) if fc == 0 else (),
                               inc=sPE if lastrd else None)
                            if lastrd:
                                fg_last[fc // 11] = mtk
                        for g2 in range(NG):
                            dmm = mm(lambda e, g3=g2, q=PB[bk], tt=t, b2=nb,
                                     dd=dcp: e.matmul(
                                q[:], c3b[g3][:, tt * 128:(tt + 1) * 128],
                                bds_sb[:, g3, b2 * 512:(b2 + 1) * 512],
                                start=False, stop=(g3 == NG - 1)),
                                waits=[c3b_toks[-1], tok_bds]
                                if g2 == 0 else [],
                                inc=sPE if g2 == NG - 1 else None,
                                stop_banks=(bk,) if g2 == NG - 1 else None)
                    prev_ht_mm[bp] = dmm
                    if dcp == 0 and t == NQT - 1:
                        wd_toks2[1] = [wd_load(1, fg, waits=[fg_last[fg]])
                                       for fg in range(4)]
                        bds_toks[1] = bds_load(1, waits=[dmm])
                    for nb in range(2):
                        bk = 2 * bp + nb
                        dc = 2 * dcp + nb
                        tev = rd("vector", lambda e, q=PB[bk], b=bp, b2=nb:
                                 e.tensor_copy(
                                     fst[b][:, b2 * 512:(b2 + 1) * 512], q[:]),
                                 waits=[prev_fst[bp]], inc=sDV, banks=(bk,))
                        prev_fst[bp] = tev
                    tfd = stb(fd[t * 128:(t + 1) * 128,
                                 dcp * 1024:(dcp + 1) * 1024],
                              fst[bp][:], waits=[prev_fst[bp]], sem=s_fd)
                    prev_fst[bp] = tfd
                    fd_w[t].append(tfd)
            b4_guard = []
            for t in range(NQT):
                fb = fbB
                x2b = x2bB
                tok_f = ld(fb[:], fd[t * 128:(t + 1) * 128, :],
                           waits=fd_w[t] + b4_guard, sem=s_fd)
                tok_x2 = ld(x2b[:], x2d[t * 128:(t + 1) * 128, :],
                            waits=x2_toks, sem=s_x2d)
                E("scalar", lambda e: e.activation(sqB[:], fb[:], AF.Identity,
                                                   accum_out=col(cB4, 0)),
                  waits=[tok_f, tok_x2] + b4_guard, inc=sAC, tiny=True)
                tsq4 = E("scalar", lambda e: e.activation(
                    sqB[:], fb[:], AF.Square, accum_out=col(cB4, 1)), inc=sAC,
                    tiny=True)
                E("vector", lambda e: e.tensor_scalar(cB4[:, 2:4], cB4[:, 0:2],
                                                      1.0 / D, None, ALU.mult),
                  waits=[tsq4], inc=sDV, tiny=True)
                E("vector", lambda e: e.tensor_tensor(col(cB4, 4), col(cB4, 2),
                                                      col(cB4, 2), op=ALU.mult),
                  inc=sDV, tiny=True)
                tv4 = E("vector", lambda e: e.tensor_tensor(
                    col(cB4, 5), col(cB4, 3), col(cB4, 4), op=ALU.subtract),
                    inc=sDV, tiny=True)
                E("scalar", lambda e: e.activation(col(cB4, 6), col(cB4, 5),
                                                   AF.Ln, bias=EPS_LN),
                  waits=[tv4], inc=sAC, tiny=True)
                trs4 = E("scalar", lambda e: e.activation(col(cB4, 7), col(cB4, 6),
                                                          AF.Exp, scale=-0.5),
                         inc=sAC, tiny=True)
                tf1 = E("vector", lambda e: e.tensor_scalar(
                    sqB[:], fb[:], col(cB4, 2), col(cB4, 7), ALU.subtract,
                    ALU.mult), waits=[trs4], inc=sDV)
                tout = E("vector", lambda e: e.tensor_tensor(
                    fb[:], sqB[:], x2b[:], op=ALU.add), inc=sDV)
                ty = stb(y[r0 + t * 128: r0 + (t + 1) * 128, :], fb[:],
                         waits=[tout], sem=sST)
                b4_guard = [ty]
            actxB2.close()
            ctxC.close()
            return a1_next, last_opro[0]
        a1 = emit_A1(0, [])
        prev_opro = None
        for chunk in range(NCH):
            a1, prev_opro = emit_chunk(chunk, a1, prev_opro)

        E("sync", lambda e: e.nop(), waits=[(sST, sST.n)])

        blk = ctx.enter_context(nc.Block())
        for s in P.sems:
            s.h = ctx.enter_context(nc.semaphore(s.name))

        def runner(my):
            def run(eng):
                ops = P.ops if max_ops is None else P.ops[:max_ops]
                for op_eng, fn, waits, inc, amt, tiny in ops:
                    if op_eng != my:
                        continue
                    for s, v in waits:
                        eng.wait_ge(s.h, v)
                    inst = fn(eng)
                    if tiny:
                        inst = eng.drain()
                    if inc is not None:
                        inst.then_inc(inc.h, amt)
            return run

        blk.tensor(runner("tensor"))
        blk.vector(runner("vector"))
        blk.scalar(runner("scalar"))
        blk.gpsimd(runner("gpsimd"))
        blk.sync(runner("sync"))

    global N_OPS_LAST
    N_OPS_LAST = len(P.ops)
    return nc, tap_aps


def kernel(**inputs):
    nc, _ = build_program()
    in_maps = [_prep_core_inputs(c, inputs) for c in range(NCORES)]
    res = run_bass_kernel_spmd(nc, in_maps, core_ids=list(range(NCORES)))
    out = np.empty((B, L, D), np.float32)
    for c in range(NCORES):
        b, half = c // 2, c % 2
        out[b, half * TC:(half + 1) * TC] = res.results[c]["y"]
    return out


# version 15
# speedup vs baseline: 1.0465x; 1.0465x over previous
import contextlib
import math
import numpy as np
import ml_dtypes
import concourse.bass as bass
import concourse.mybir as mybir
from concourse.bass_utils import run_bass_kernel_spmd

F32 = mybir.dt.float32
BF16 = mybir.dt.bfloat16
AF = mybir.ActivationFunctionType
ALU = mybir.AluOpType
AX = mybir.AxisListType
BF = ml_dtypes.bfloat16

B, L, D = 4, 4096, 2048
H, DH, RLO, WIN = 16, 128, 16, 512
DFF, NE = 5632, 8
NCORES = 8
TC, CH, HALO = 2048, 1024, 512
NCH = TC // CH
KV = CH + HALO
NQT, NKT, NIC = CH // 128, KV // 128, D // 128
EPS_LN, EPS_RMS = 1e-5, 1e-6
NEG = -1.0e30
ISD = 1.0 / math.sqrt(DH)
GROUPS = [(0, 1, 2), (3, 4, 5), (6, 7)]
NG = len(GROUPS)
FFC = 256
NFFC = DFF // FFC
NFT = DFF // 128
SCALING = 2.0
LORA_SCALE = 1.0 / 16.0
ENGS = ("tensor", "vector", "scalar", "gpsimd", "sync")
N_OPS_LAST = 0


class Sem:
    __slots__ = ("name", "n", "h", "dma")

    def __init__(self, name, dma=False):
        self.name, self.n, self.h, self.dma = name, 0, None, dma


class Plan:
    def __init__(self):
        self.ops = []
        self.sems = []
        self.mark = {e: {} for e in ENGS}

    def sem(self, name, dma=False):
        s = Sem(name, dma)
        self.sems.append(s)
        return s

    def op(self, eng, fn, waits=(), inc=None, amt=1, is_dma=False, tiny=False):
        assert eng in ENGS, eng
        wl = {}
        for t in waits:
            if t is None:
                continue
            s, v = t
            if s.dma:
                v = s.n
            assert v <= s.n, f"[{eng}] wait {s.name}>={v} but only {s.n} planned"
            if v > self.mark[eng].get(s, 0):
                wl[s] = max(wl.get(s, 0), v)
        for s, v in wl.items():
            self.mark[eng][s] = v
        tok = None
        if inc is not None:
            inc.n += amt
            tok = (inc, inc.n)
            if not is_dma:
                self.mark[eng][inc] = max(self.mark[eng].get(inc, 0), inc.n)
        self.ops.append((eng, fn, tuple(wl.items()), inc, amt, tiny))
        return tok


def _rope_tables():
    half = DH // 2
    theta = 1.0 / 10000.0 ** (np.arange(half, dtype=np.float32) / half)
    fr = np.arange(L, dtype=np.float32)[:, None] * theta[None, :]
    emb = np.concatenate([fr, fr], axis=-1)
    return np.cos(emb).astype(np.float32), np.sin(emb).astype(np.float32)


def _band_mask(d):
    r = np.arange(128)[:, None]
    c = np.arange(128)[None, :]
    diff = d + r - c
    return np.where((diff >= 0) & (diff <= WIN - 1), 0.0, NEG).astype(np.float32)


def _start_masks(first_half):
    out = np.zeros((4, 5, 128, 128), np.float32)
    r = np.arange(128)[:, None]
    c = np.arange(128)[None, :]
    for qt in range(4):
        q0 = 128 * qt
        for kt in range(5):
            j = (q0 - 512 + 128 * kt) + c
            q = q0 + r
            valid = (j <= q) & (j >= q - (WIN - 1))
            if first_half:
                valid = valid & (j >= 0)
            out[qt, kt] = np.where(valid, 0.0, NEG)
    return out


def _prep_core_inputs(core, full):
    b, half = core // 2, core % 2
    t0 = half * TC
    g = lambda k: np.ascontiguousarray(np.asarray(full[k], np.float32))
    x = np.asarray(full["L0_attn_x"])
    if half == 0:
        xs = np.concatenate([np.zeros((HALO, D), np.float32),
                             np.asarray(x[b, 0:TC], np.float32)], axis=0)
    else:
        xs = np.ascontiguousarray(np.asarray(x[b, t0 - HALO:t0 + TC], np.float32))
    cos, sin = _rope_tables()
    pos = np.clip(np.arange(-HALO, TC) + t0, 0, L - 1)
    d = {
        "xs": xs,
        "cosf": np.ascontiguousarray(cos[pos].T.astype(BF)),
        "sinf": np.ascontiguousarray(sin[pos].T.astype(BF)),
        "wqT": np.ascontiguousarray(g("L0_attn_wq").T),
        "wkT": np.ascontiguousarray(g("L0_attn_wk").T),
        "wvT": np.ascontiguousarray(g("L0_attn_wv").T),
        "woT": np.ascontiguousarray(g("L0_attn_wo").T),
        "qAT": np.ascontiguousarray(g("L0_attn_qA").T),
        "kAT": np.ascontiguousarray(g("L0_attn_kA").T),
        "vAT": np.ascontiguousarray(g("L0_attn_vA").T),
        "oAT": np.ascontiguousarray(g("L0_attn_oA").T),
        "qBT": np.ascontiguousarray(g("L0_attn_qB").T),
        "kBT": np.ascontiguousarray(g("L0_attn_kB").T),
        "vBT": np.ascontiguousarray(g("L0_attn_vB").T),
        "oBT": np.ascontiguousarray(g("L0_attn_oB").T),
        "normw": np.ascontiguousarray(g("L0_attn_norm_w").reshape(NIC, 128).T),
        "mask_std": np.stack([_band_mask(WIN), _band_mask(0)]),
        "mask_start": _start_masks(half == 0),
        "identb": np.eye(128, dtype=BF),
        "cvals": np.broadcast_to(np.array(
            [EPS_LN, EPS_RMS, -0.5, -1.0, ISD, 0.0, 0.0, 0.0], np.float32),
            (128, 8)).copy(),
        "identf": np.eye(128, dtype=np.float32),
        "WrT": np.ascontiguousarray(g("L0_ffn_Wr").T),
        "WgT": np.ascontiguousarray(g("L0_ffn_Wg").T),
        "WuT": np.ascontiguousarray(g("L0_ffn_Wu").T),
        "WdT": np.ascontiguousarray(g("L0_ffn_Wd").T),
    }
    Ag, Au = g("L0_ffn_Ag"), g("L0_ffn_Au")
    Bg, Bu = g("L0_ffn_Bg"), g("L0_ffn_Bu")
    Ad, Bd = g("L0_ffn_Ad"), g("L0_ffn_Bd")
    AgS = np.zeros((NG, D, 128), np.float32)
    AuS = np.zeros((NG, D, 128), np.float32)
    BgP = np.zeros((NG, 128, DFF), np.float32)
    BuP = np.zeros((NG, 128, DFF), np.float32)
    Ad8 = np.zeros((NE, DFF, 128), np.float32)
    BdS = np.zeros((NG, 128, D), np.float32)
    REP = np.zeros((NG, 8, 128), np.float32)
    for gi, exps in enumerate(GROUPS):
        for j, e in enumerate(exps):
            AgS[gi, :, 32 * j:32 * j + 16] = Ag[e].T
            AuS[gi, :, 32 * j:32 * j + 16] = Au[e].T
            BgP[gi, 32 * j:32 * j + 16, :] = Bg[e].T
            BuP[gi, 32 * j:32 * j + 16, :] = Bu[e].T
            Ad8[e, :, 32 * j:32 * j + 16] = Ad[e].T
            BdS[gi, 32 * j:32 * j + 16, :] = Bd[e].T
            REP[gi, e, 32 * j:32 * j + 16] = SCALING
    BgE = np.stack([np.ascontiguousarray(Bg[e].T) for e in range(NE)])
    BuE = np.stack([np.ascontiguousarray(Bu[e].T) for e in range(NE)])
    d.update(AgS=AgS, AuS=AuS, BgP=BgP, BuP=BuP, Ad8=Ad8, BdS=BdS, REP=REP,
             BgE=BgE, BuE=BuE)
    return d


def build_program(taps=(), max_ops=None):
    taps = set(taps)
    nc = bass.Bass("TRN2", target_bir_lowering=False)

    di = {}
    def din(name, shape, dtype=F32):
        di[name] = nc.dram_tensor(name, list(shape), dtype, kind="ExternalInput").ap()
    din("xs", (HALO + TC, D))
    din("cosf", (128, HALO + TC), BF16); din("sinf", (128, HALO + TC), BF16)
    for w in ("wqT", "wkT", "wvT", "woT"):
        din(w, (D, D))
    for w in ("qAT", "kAT", "vAT", "oAT"):
        din(w, (D, RLO))
    for w in ("qBT", "kBT", "vBT", "oBT"):
        din(w, (RLO, D))
    din("normw", (128, NIC))
    din("mask_std", (2, 128, 128))
    din("mask_start", (4, 5, 128, 128))
    din("identb", (128, 128), BF16)
    din("cvals", (128, 8))
    din("identf", (128, 128))
    din("WrT", (D, NE))
    din("WgT", (D, DFF)); din("WuT", (D, DFF)); din("WdT", (DFF, D))
    din("AgS", (NG, D, 128)); din("AuS", (NG, D, 128))
    din("BgP", (NG, 128, DFF)); din("BuP", (NG, 128, DFF))
    din("Ad8", (NE, DFF, 128)); din("BdS", (NG, 128, D))
    din("BgE", (NE, RLO, DFF)); din("BuE", (NE, RLO, DFF))
    din("REP", (NG, 8, 128))

    y = nc.dram_tensor("y", [TC, D], F32, kind="ExternalOutput").ap()
    tdram = nc.dram_tensor("tdram", [KV + CH, D], BF16, kind="Internal").ap()
    x2d = nc.dram_tensor("x2d", [CH, D], F32, kind="Internal").ap()
    hbd = nc.dram_tensor("hbd", [DFF, CH], BF16, kind="Internal").ap()
    fd = nc.dram_tensor("fd", [CH, D], F32, kind="Internal").ap()

    tap_aps = {}
    def mktap(name, shape, dtype=F32):
        if name in taps:
            tap_aps[name] = nc.dram_tensor("tap_" + name, list(shape), dtype,
                                           kind="ExternalOutput").ap()
    mktap("h_fm", (128, NIC, KV), BF16)
    mktap("q_h0", (128, CH), BF16)
    mktap("k_h0", (128, KV), BF16)
    mktap("v_h0", (128, NKT, 128), BF16)
    mktap("p_h0", (128, 640), BF16)
    mktap("o_fm", (128, H, CH), BF16)
    mktap("x2", (TC, D))
    mktap("ln3", (128, NIC, CH), BF16)
    mktap("gu0", (128, 2, CH))
    mktap("c3", (128, NG, CH))
    mktap("werm", (8, CH))
    mktap("hb0", (128, 2, CH), BF16)
    def tap(name):
        return tap_aps.get(name)

    P = Plan()
    d = True
    s_x = [P.sem("s_x0", d), P.sem("s_x1", d)]
    s_w = [P.sem("s_w0", d), P.sem("s_w1", d)]
    s_h2 = [P.sem("s_h0", d), P.sem("s_h1", d)]
    s_mi = P.sem("s_mi", d)
    s_wm = P.sem("s_wm", d)
    s_tw = P.sem("s_tw", d)
    sTR = P.sem("sTR", d)
    s_x2d = P.sem("s_x2d", d)
    s_hb = P.sem("s_hb", d)
    s_fd = P.sem("s_fd", d)
    s_tap = P.sem("s_tap", d)
    s_cs = P.sem("s_cs", d)
    s_wd = [P.sem(f"s_wd{i}", d) for i in range(4)]
    s_h3 = [P.sem("s_h30", d), P.sem("s_h31", d)]
    sST = P.sem("sST", d)
    sPE = P.sem("sPE"); sDV = P.sem("sDV"); sAC = P.sem("sAC")
    ALLS = tuple(P.sems)
    sDV2 = P.sem("sDV2"); sAC2 = P.sem("sAC2")
    s_xA = [P.sem("s_xA0", d), P.sem("s_xA1", d)]
    s_twA = P.sem("s_twA", d)

    PBw = [None] * 8
    PBr = [[] for _ in range(8)]

    ctx = contextlib.ExitStack()
    with ctx:
        def sb(name, shape, dtype):
            return ctx.enter_context(nc.sbuf_tensor(name, list(shape), dtype))
        PB = [ctx.enter_context(nc.psum_tensor(f"PB{i}", [128, 512], F32))
              for i in range(8)]

        xt = [sb("xt0", (128, D), F32), sb("xt1", (128, D), F32)]
        sq = sb("sq", (128, D), F32)
        ident_b = sb("ident_b", (128, 128), BF16)
        ident_f = sb("ident_f", (128, 128), F32)
        normw = sb("normw_sb", (128, NIC), F32)
        mstd = sb("mstd", (128, 2, 128), F32)
        m1c = sb("m1c", (128, NQT), F32)
        rs1c = sb("rs1c", (128, NQT), F32)
        rowsum = sb("rowsum", (128, 2), F32)
        invs = sb("invs", (128, 2), F32)
        cA = sb("cA", (128, 16), F32)
        cB = sb("cB", (128, 16), F32)
        cC = sb("cC", (128, 16), F32)
        cD = sb("cD", (128, 16), F32)
        wr_sb = sb("wr_sb", (128, NIC, NE), BF16)
        rep_sb = sb("rep_sb", (8, NG, 128), F32)
        rt_a = sb("rt_a", (128, NE), F32)
        rt_b = sb("rt_b", (128, NE), F32)
        hbuf = [sb("hbuf0", (128, D), BF16), sb("hbuf1", (128, D), BF16)]
        cvals = sb("cvals_sb", (128, 8), F32)
        for _i, _v in enumerate([EPS_LN, EPS_RMS, -0.5, -1.0, ISD]):
            nc.const_aps.register(F32, _v, cvals[:, _i:_i + 1])

        E = P.op

        def mm(fn, waits=(), inc=None, wbanks=(), stop_banks=None):
            w = list(waits) + [PBw[i] for i in wbanks]
            for i in wbanks:
                w += PBr[i]
            tok = E("tensor", fn, waits=w, inc=inc)
            if stop_banks:
                for i in stop_banks:
                    PBw[i] = tok
                    PBr[i] = []
            return tok

        def rd(eng, fn, waits=(), inc=None, banks=(), tiny=False):
            w = list(waits) + [PBw[i] for i in banks]
            tok = E(eng, fn, waits=w, inc=inc, tiny=tiny)
            for i in banks:
                PBr[i].append(tok)
            return tok

        def ld(dst, src, waits=(), sem=None):
            return E("sync", lambda e, dd=dst, s=src: e.dma_start(out=dd, in_=s),
                     waits=waits, inc=sem or s_mi, amt=16, is_dma=True)

        def stb(dst, src, waits=(), sem=None):
            return E("sync", lambda e, dd=dst, s=src: e.dma_start(out=dd, in_=s),
                     waits=waits, inc=sem or s_mi, amt=16, is_dma=True)

        def wcb(dst, src, waits=(), sem=None):
            return E("gpsimd", lambda e, dd=dst, s=src: e.dma_start(out=dd, in_=s),
                     waits=waits, inc=sem or s_wm, amt=16, is_dma=True)

        def trp(dst, src, waits=()):
            return E("sync", lambda e, dd=dst, s=src: e.dma_start_transpose(dd, s),
                     waits=waits, inc=sTR, amt=16, is_dma=True)

        def col(t, i):
            return t[:, i:i + 1]

        def barrier():
            toks = [(s, s.n) for s in ALLS]
            return [E("tensor", lambda e: e.nop(), waits=toks, inc=sPE),
                    E("vector", lambda e: e.engine_nop(), waits=toks, inc=sDV),
                    E("scalar", lambda e: e.nop(), waits=toks, inc=sAC),
                    E("gpsimd", lambda e: e.nop(), waits=toks),
                    E("sync", lambda e: e.nop(), waits=toks)]

        tok_cv = ld(cvals[:], di["cvals"])
        tok_ident = ld(ident_b[:], di["identb"])
        tok_identf = ld(ident_f[:], di["identf"])
        tok_normw = ld(normw[:], di["normw"])
        tok_mstd = ld(mstd[:], di["mask_std"].rearrange("a b c -> b a c"))
        tok_rep = ld(rep_sb[:], di["REP"].rearrange("a b c -> b a c"))
        rr = lambda ap: ap.rearrange("(c p) r -> p c r", p=128)
        tok_wr = wcb(wr_sb[:], rr(di["WrT"]))
        E("scalar", lambda e: e.nop(), waits=[tok_cv], inc=sAC)

        def stats(xtile, wtoks, want_sh, dst_m=None, dst_rs=None,
                  sD=None, sA=None):
            sD = sD or sDV
            sA = sA or sAC
            E("scalar", lambda e: e.activation(sq[:], xtile[:], AF.Identity,
                                               accum_out=col(cA, 0)),
              waits=wtoks, inc=sA, tiny=True)
            tsq = E("scalar", lambda e: e.activation(sq[:], xtile[:], AF.Square,
                                                     accum_out=col(cA, 1)),
                    inc=sA, tiny=True)
            E("vector", lambda e: e.tensor_scalar(cB[:, 0:2], cA[:, 0:2], 1.0 / D,
                                                  None, ALU.mult),
              waits=[tsq], inc=sD, tiny=True)
            E("vector", lambda e: e.tensor_tensor(col(cC, 0), col(cB, 0), col(cB, 0),
                                                  op=ALU.mult), inc=sD, tiny=True)
            tv = E("vector", lambda e: e.tensor_tensor(col(cC, 1), col(cB, 1),
                                                       col(cC, 0), op=ALU.subtract),
                   inc=sD, tiny=True)
            E("scalar", lambda e: e.activation(col(cD, 0), col(cC, 1), AF.Ln,
                                               bias=EPS_LN), waits=[tv], inc=sA,
              tiny=True)
            trs = E("scalar", lambda e: e.activation(col(cD, 1), col(cD, 0), AF.Exp,
                                                     scale=-0.5), inc=sA, tiny=True)
            toks = {"rs": trs}
            if dst_m is not None:
                E("vector", lambda e: e.tensor_copy(dst_m, col(cB, 0)),
                  waits=[trs], inc=sD, tiny=True)
                toks["dst"] = E("vector", lambda e: e.tensor_copy(dst_rs, col(cD, 1)),
                                inc=sD, tiny=True)
            if not want_sh:
                return toks
            E("vector", lambda e: e.tensor_tensor(col(cC, 2), col(cD, 1), col(cD, 1),
                                                  op=ALU.mult), waits=[trs], inc=sD,
              tiny=True)
            t3 = E("vector", lambda e: e.tensor_tensor(col(cC, 3), col(cC, 2),
                                                       col(cC, 1), op=ALU.mult),
                   inc=sD, tiny=True)
            E("scalar", lambda e: e.activation(col(cD, 2), col(cC, 3), AF.Ln,
                                               bias=EPS_RMS), waits=[t3], inc=sA,
              tiny=True)
            tr2 = E("scalar", lambda e: e.activation(col(cD, 3), col(cD, 2), AF.Exp,
                                                     scale=-0.5), inc=sA, tiny=True)
            E("vector", lambda e: e.tensor_tensor(col(cC, 4), col(cD, 1), col(cD, 3),
                                                  op=ALU.mult), waits=[tr2], inc=sD,
              tiny=True)
            toks["sh"] = E("vector", lambda e: e.scalar_tensor_tensor(
                col(cC, 5), col(cB, 0), -1.0, col(cC, 4), ALU.mult, ALU.mult),
                inc=sD, tiny=True)
            return toks

        def emit_A1(chunk, pre):
            r0 = chunk * CH
            hwr_toks = []
            last_hpass = None
            xt_guard = [list(pre), list(pre)]
            for kt in range(NKT):
                xb = xt[kt % 2]
                hx = hbuf[kt % 2]
                tok_x = ld(xb[:], di["xs"][r0 + kt * 128: r0 + (kt + 1) * 128, :],
                           waits=pre + xt_guard[kt % 2], sem=s_xA[kt % 2])
                own = kt - (HALO // 128)
                w = [tok_x] + ([last_hpass] if last_hpass else list(pre))
                if 0 <= own < NQT:
                    tk = stats(xb, w, True, dst_m=col(m1c, own),
                               dst_rs=col(rs1c, own), sD=sDV2, sA=sAC2)
                else:
                    tk = stats(xb, w, True, sD=sDV2, sA=sAC2)
                hbuf_guard = [hwr_toks[kt - 2]] if kt >= 2 else []
                last_hpass = E("scalar", lambda e, o=hx, i=xb:
                               e.activation(o[:], i[:], AF.Identity, bias=col(cC, 5),
                                            scale=col(cC, 4)),
                               waits=[tk["sh"]] + hbuf_guard, inc=sAC2)
                hwr_toks.append(stb(tdram[kt * 128:(kt + 1) * 128, :], hx[:],
                                    waits=[last_hpass], sem=s_twA))
                xt_guard[kt % 2] = [last_hpass]
            return {"hwr": hwr_toks, "xt_guard": xt_guard, "hp": last_hpass}

        def emit_chunk(chunk, a1, prev_opro=None):
            r0 = chunk * CH
            hwr_toks = a1["hwr"]
            actxA = contextlib.ExitStack()
            sbA = lambda n, s, d: actxA.enter_context(nc.sbuf_tensor(f"{n}_c{chunk}", list(s), d))
            actxO = contextlib.ExitStack()
            O_fm = actxO.enter_context(
                nc.sbuf_tensor(f"O_fm_c{chunk}", [128, H, CH], BF16,
                               side="right"))
            h_fm = actxA.enter_context(
                nc.sbuf_tensor(f"h_fm_c{chunk}", [128, NIC, KV], BF16,
                               side="right"))
            cosf = sbA("cosf_sb", (128, KV), BF16)
            sinf = sbA("sinf_sb", (128, KV), BF16)
            mstart = sbA("mstart", (128, 20, 128), F32)

            bar = barrier() if chunk > 0 else []
            if chunk > 0:
                a1toks = [(sDV2, sDV2.n), (sAC2, sAC2.n),
                          (s_xA[0], s_xA[0].n), (s_xA[1], s_xA[1].n),
                          (s_twA, s_twA.n)]
                bar = bar + [E("tensor", lambda e: e.nop(), waits=a1toks, inc=sPE),
                             E("vector", lambda e: e.engine_nop(), waits=a1toks,
                               inc=sDV)]
            tok_cos = ld(cosf[:], di["cosf"][:, r0:r0 + KV], waits=bar, sem=s_cs)
            tok_sin = ld(sinf[:], di["sinf"][:, r0:r0 + KV], sem=s_cs)
            tok_mstart = ld(mstart[:],
                            di["mask_start"].rearrange("a b c d -> c (a b) d"),
                            sem=s_cs)
            hfm_toks = []
            ttr = None
            for ic in range(NIC):
                ttr = trp(h_fm[:, ic, :], tdram[0:KV, ic * 128:(ic + 1) * 128],
                          waits=hwr_toks + ([prev_opro] if prev_opro else []))
            for ic in range(NIC):
                hfm_toks.append(
                    E("vector", lambda e, i=ic: e.tensor_scalar(
                        h_fm[:, i, :], h_fm[:, i, :], col(normw, i), None, ALU.mult),
                      waits=[ttr, tok_normw], inc=sDV))
            a2_tr_val = (sTR, sTR.n)
            if chunk == 0 and tap("h_fm") is not None:
                stb(tap("h_fm"), h_fm[:], waits=hfm_toks, sem=s_tap)
            wqs = [sbA(f"wqs{i}", (128, NIC, 128), BF16) for i in range(2)]
            wks = [sbA(f"wks{i}", (128, NIC, 128), BF16) for i in range(2)]
            wvs = [sbA(f"wvs{i}", (128, NIC, 128), BF16) for i in range(2)]
            qA_sb = sbA("qA_sb", (128, NIC, RLO), BF16)
            kA_sb = sbA("kA_sb", (128, NIC, RLO), BF16)
            vA_sb = sbA("vA_sb", (128, NIC, RLO), BF16)
            qB_sb = sbA("qB_sb", (RLO, D), BF16)
            kB_sb = sbA("kB_sb", (RLO, D), BF16)
            vB_sb = sbA("vB_sb", (RLO, D), BF16)
            a_q = sbA("a_q", (RLO, CH), BF16)
            a_k = sbA("a_k", (RLO, KV), BF16)
            a_v = sbA("a_v", (RLO, KV), BF16)
            K_h = sbA("K_h", (128, KV), BF16)
            V_fmsb = sbA("V_fmsb", (128, KV), BF16)
            V_tm = sbA("V_tm", (128, NKT, 128), BF16)
            Q_h = sbA("Q_h", (128, CH), BF16)
            rop0 = sbA("rop0", (128, 512), F32)
            rop1 = sbA("rop1", (128, 512), F32)
            S_sb2 = sbA("S_sb", (128, 2, 640), F32)
            P_sb2 = sbA("P_sb", (128, 2, 640), BF16)
            PT_sb2 = sbA("PT_sb", (128, 2, 5, 128), BF16)
            tok_qA = wcb(qA_sb[:], rr(di["qAT"]), waits=hwr_toks)
            tok_kA = wcb(kA_sb[:], rr(di["kAT"]))
            tok_vA = wcb(vA_sb[:], rr(di["vAT"]))
            tok_qB = wcb(qB_sb[:], di["qBT"])
            tok_kB = wcb(kB_sb[:], di["kBT"])
            tok_vB = wcb(vB_sb[:], di["vBT"])

            def avec(dst, Asb, n_tok, base, wtok):
                toks = []
                for i, n0 in enumerate(range(0, n_tok, 512)):
                    bk = i % 2
                    pb = PB[bk]
                    for ic in range(NIC):
                        mm(lambda e, j=ic, p=pb, nn=n0, s=(ic == 0),
                           q=(ic == NIC - 1): e.matmul(
                               p[0:16, :], Asb[:, j, :],
                               h_fm[:, j, base + nn: base + nn + 512],
                               start=s, stop=q),
                           waits=([wtok] + hfm_toks) if ic == 0 else [],
                           inc=sPE if ic == NIC - 1 else None,
                           wbanks=(bk,) if ic == 0 else (),
                           stop_banks=(bk,) if ic == NIC - 1 else None)
                    toks.append(rd("vector", lambda e, p=pb, nn=n0: e.tensor_scalar(
                        dst[:, nn:nn + 512], p[0:16, :], LORA_SCALE, None, ALU.mult),
                        inc=sDV, banks=(bk,)))
                return toks
            aq_toks = avec(a_q, qA_sb, CH, HALO, tok_qA)
            ak_toks = avec(a_k, kA_sb, KV, 0, tok_kA)
            av_toks = avec(a_v, vA_sb, KV, 0, tok_vA)

            wtoks = [None, None]
            def head_w(h, guard=()):
                p = h % 2
                sl = lambda w: di[w][:, h * 128:(h + 1) * 128].rearrange(
                    "(c p) m -> p c m", p=128)
                wtoks[p] = [wcb(wqs[p][:], sl("wqT"), waits=guard, sem=s_w[p]),
                            wcb(wks[p][:], sl("wkT"), sem=s_w[p]),
                            wcb(wvs[p][:], sl("wvT"), sem=s_w[p])]
            head_w(0)
            o_evac_toks = []
            prev_rope_guard = None
            prev_attn_pe = None
            for h in range(H):
                p = h % 2
                wt = wtoks[p]
                if h + 1 < H:
                    head_w(h + 1, guard=[prev_attn_pe] if prev_attn_pe else [])

                def proj(wsb, Bsb, av_sb, av_tk, n_tok, base, banks, extra):
                    toks = []
                    for i, n0 in enumerate(range(0, n_tok, 512)):
                        bk = banks[i]
                        pb = PB[bk]
                        for ic in range(NIC):
                            mm(lambda e, j=ic, q=pb, nn=n0: e.matmul(
                                q[:], wsb[:, j, :],
                                h_fm[:, j, base + nn: base + nn + 512],
                                start=(j == 0), stop=False),
                               waits=(wt + extra + hfm_toks + av_tk)
                               if ic == 0 else [],
                               wbanks=(bk,) if ic == 0 else ())
                        toks.append(mm(lambda e, q=pb, nn=n0, hh=h: e.matmul(
                            q[:], Bsb[0:16, hh * 128:(hh + 1) * 128],
                            av_sb[0:16, nn:nn + 512], start=False, stop=True),
                            inc=sPE, stop_banks=(bk,)))
                    return toks
                q_mm = proj(wqs[p], qB_sb, a_q, aq_toks, CH, HALO, (0, 1), [tok_qB])
                k_mm = proj(wks[p], kB_sb, a_k, ak_toks, KV, 0, (2, 3, 4), [tok_kB])
                v_mm = proj(wvs[p], vB_sb, a_v, av_toks, KV, 0, (5, 6, 7), [tok_vB])

                def rope(banks, dst, n_tok, coff, guard, ctab, stab):
                    toks = []
                    for i, n0 in enumerate(range(0, n_tok, 512)):
                        bk = banks[i]
                        pb = PB[bk]
                        cs = slice(coff + n0, coff + n0 + 512)
                        g = [guard] if guard else []
                        if toks:
                            g = [toks[-1]]
                        rd("scalar", lambda e, q=pb: e.activation(
                            rop0[0:64, :], q[64:128, :], AF.Copy, scale=-1.0),
                           waits=g, inc=sAC, banks=(bk,))
                        t2 = rd("scalar", lambda e, q=pb: e.activation(
                            rop0[64:128, :], q[0:64, :], AF.Copy), inc=sAC,
                           banks=(bk,))
                        E("vector", lambda e, c=cs, st_=stab: e.tensor_tensor(
                            rop1[:], rop0[:], st_[:, c], op=ALU.mult),
                          waits=[t2, tok_sin], inc=sDV)
                        t4 = rd("vector", lambda e, q=pb, c=cs, ct=ctab:
                                e.tensor_tensor(
                                    rop0[:], q[:], ct[:, c], op=ALU.mult),
                                waits=[tok_cos], inc=sDV, banks=(bk,))
                        toks.append(E("gpsimd", lambda e, z=slice(n0, n0 + 512),
                                      dd=dst: e.tensor_tensor(
                                          dd[:, z], rop0[:], rop1[:], op=ALU.add),
                                      waits=[prev_attn_pe, t4], inc=sDV))
                    return toks
                tq = rope((0, 1), Q_h, CH, HALO, prev_rope_guard, cosf, sinf)
                tk_r = rope((2, 3, 4), K_h, KV, 0, tq[-1], cosf, sinf)
                prev_rope_guard = tk_r[-1]
                tv = []
                for i, n0 in enumerate(range(0, KV, 512)):
                    bk = (5, 6, 7)[i]
                    tv.append(rd("scalar", lambda e, q=PB[bk], nn=n0:
                                 e.activation(V_fmsb[:, nn:nn + 512], q[:],
                                              AF.Identity),
                                 waits=[prev_attn_pe], inc=sAC, banks=(bk,)))
                vt_toks = []
                for jt in range(NKT):
                    bk = 5 + jt % 2
                    tmm = mm(lambda e, j=jt, q=PB[bk]: e.matmul(
                        q[:, 0:128], V_fmsb[:, j * 128:(j + 1) * 128], ident_b[:],
                        start=True, stop=True),
                        waits=[tv[jt // 4], tok_ident], wbanks=(bk,), inc=sPE,
                        stop_banks=(bk,))
                    vt_toks.append(rd("scalar", lambda e, j=jt, q=PB[bk]:
                                      e.activation(V_tm[:, j, :], q[:, 0:128],
                                                   AF.Identity),
                                      inc=sAC, banks=(bk,)))
                if chunk == 0 and h == 0:
                    if tap("q_h0") is not None:
                        stb(tap("q_h0"), Q_h[:], waits=tq, sem=s_tap)
                    if tap("k_h0") is not None:
                        stb(tap("k_h0"), K_h[:], waits=tk_r, sem=s_tap)
                    if tap("v_h0") is not None:
                        stb(tap("v_h0"), V_tm[:], waits=vt_toks, sem=s_tap)

                def s_mms(qt):
                    pa = qt % 2
                    b0, b1 = (2, 3) if pa == 0 else (0, 1)
                    t1 = mm(lambda e, q=qt, bb=b0: e.matmul(
                        PB[bb][:], Q_h[:, q * 128:(q + 1) * 128],
                        K_h[:, q * 128:q * 128 + 512], start=True, stop=True),
                       waits=tq + tk_r + vt_toks, wbanks=(b0,), inc=sPE,
                       stop_banks=(b0,))
                    t2 = mm(lambda e, q=qt, bb=b1: e.matmul(
                        PB[bb][:, 0:128], Q_h[:, q * 128:(q + 1) * 128],
                        K_h[:, q * 128 + 512:q * 128 + 640], start=True,
                        stop=True),
                       wbanks=(b1,), inc=sPE, stop_banks=(b1,))
                    return (t1, t2)

                def softmax_qt(qt):
                    pa = qt % 2
                    b0, b1 = (2, 3) if pa == 0 else (0, 1)
                    S_sb = S_sb2[:, pa, :]
                    P_sb = P_sb2[:, pa, :]
                    ev = []
                    if chunk == 0 and qt < 4:
                        for k5 in range(4):
                            ev.append(rd("vector", lambda e, k=k5, q=qt, bb=b0,
                                         ss=S_sb: e.scalar_tensor_tensor(
                                             ss[:, k * 128:(k + 1) * 128],
                                             PB[bb][:, k * 128:(k + 1) * 128],
                                             ISD, mstart[:, q * 5 + k, :],
                                             ALU.mult, ALU.add),
                                         waits=[tok_mstart], inc=sDV,
                                         banks=(b0,)))
                        ev.append(rd("vector", lambda e, q=qt, bb=b1, ss=S_sb:
                                     e.scalar_tensor_tensor(
                                         ss[:, 512:640], PB[bb][:, 0:128], ISD,
                                         mstart[:, q * 5 + 4, :],
                                         ALU.mult, ALU.add),
                                     inc=sDV, banks=(b1,)))
                    else:
                        ev.append(rd("vector", lambda e, bb=b0, ss=S_sb:
                                     e.scalar_tensor_tensor(
                            ss[:, 0:128], PB[bb][:, 0:128], ISD, mstd[:, 0, :],
                            ALU.mult, ALU.add),
                            waits=[tok_mstd], inc=sDV, banks=(b0,)))
                        ev.append(rd("vector", lambda e, bb=b0, ss=S_sb:
                                     e.tensor_scalar(
                            ss[:, 128:512], PB[bb][:, 128:512], ISD, None,
                            ALU.mult), inc=sDV, banks=(b0,)))
                        ev.append(rd("vector", lambda e, bb=b1, ss=S_sb:
                                     e.scalar_tensor_tensor(
                            ss[:, 512:640], PB[bb][:, 0:128], ISD, mstd[:, 1, :],
                            ALU.mult, ALU.add), inc=sDV, banks=(b1,)))
                    tbx = E("vector", lambda e, ss=S_sb, p2=pa: e.tensor_reduce(
                        col(cA, 8 + p2), ss[:], axis=AX.X, op=ALU.max,
                        negate=True), waits=ev, inc=sDV, tiny=True)
                    texp = E("scalar", lambda e, ss=S_sb, pp=P_sb, p2=pa:
                             e.activation(pp[:], ss[:], AF.Exp,
                                          bias=col(cA, 8 + p2),
                                          accum_out=col(rowsum, p2)),
                             waits=[tbx], inc=sAC, tiny=True)
                    ti2 = E("vector", lambda e, p2=pa: e.reciprocal(
                        col(invs, p2), col(rowsum, p2)),
                            waits=[texp], inc=sDV, tiny=True)
                    tps = E("vector", lambda e, pp=P_sb, p2=pa: e.tensor_scalar(
                        pp[:], pp[:], col(invs, p2), None, ALU.mult),
                        waits=[ti2], inc=sDV)
                    if chunk == 0 and h == 0 and qt == 4 and tap("p_h0") is not None:
                        tps = stb(tap("p_h0"), P_sb[:], waits=[tps], sem=s_tap)
                    return tps

                def ptpv(qt, tps):
                    nonlocal prev_attn_pe
                    pa = qt % 2
                    P_sb = P_sb2[:, pa, :]
                    PT_sb = PT_sb2[:, pa, :, :]
                    pt_toks = []
                    for k5 in range(5):
                        bk = 4 + (k5 % 2)
                        mm(lambda e, k=k5, q=PB[bk], pp=P_sb: e.matmul(
                            q[:, 0:128], pp[:, k * 128:(k + 1) * 128], ident_b[:],
                            start=True, stop=True),
                           waits=[tps], wbanks=(bk,), inc=sPE, stop_banks=(bk,))
                        pt_toks.append(rd("scalar", lambda e, k=k5, q=PB[bk],
                                          pt_=PT_sb: e.activation(
                                              pt_[:, k, :], q[:, 0:128],
                                              AF.Identity),
                                          inc=sAC, banks=(bk,)))
                    o_mm = None
                    for k5 in range(5):
                        o_mm = mm(lambda e, k=k5, q=qt, pt_=PT_sb: e.matmul(
                            PB[6][:, 0:128], V_tm[:, q + k, :], pt_[:, k, :],
                            start=(k == 0), stop=(k == 4)),
                            waits=[pt_toks[k5]], wbanks=(6,) if k5 == 0 else (),
                            inc=sPE if k5 == 4 else None,
                            stop_banks=(6,) if k5 == 4 else None)
                    prev_attn_pe = o_mm
                    o_evac_toks.append(rd("scalar", lambda e, hh=h, q=qt:
                                          e.activation(
                                              O_fm[:, hh, q * 128:(q + 1) * 128],
                                              PB[6][:, 0:128], AF.Identity),
                                          inc=sAC, banks=(6,)))

                s_mms(0)
                prev_tps = None
                for qt in range(1, NQT):
                    s_mms(qt)
                    prev_tps = softmax_qt(qt - 1)
                    ptpv(qt - 1, prev_tps)
                prev_tps = softmax_qt(NQT - 1)
                ptpv(NQT - 1, prev_tps)
            if chunk == 0 and tap("o_fm") is not None:
                stb(tap("o_fm"), O_fm[:], waits=o_evac_toks, sem=s_tap)

            actxA.close()
            ctxC = contextlib.ExitStack()
            c3 = [ctxC.enter_context(nc.sbuf_tensor(f"c3_{g}_c{chunk}",
                                                    [128, CH], F32))
                  for g in range(NG)]
            we_rm = ctxC.enter_context(
                nc.sbuf_tensor(f"we_rm_c{chunk}", [8, CH], F32))
            ctxL = contextlib.ExitStack()
            ln3_fm = ctxL.enter_context(
                nc.sbuf_tensor(f"ln3_fm_c{chunk}", [128, NIC, CH], BF16))
            actx4 = contextlib.ExitStack()
            sb4 = lambda n, s, d: actx4.enter_context(nc.sbuf_tensor(f"{n}_c{chunk}", list(s), d))
            wo_full = sb4("wo_full", (128, NIC, D), BF16)
            a_o = sb4("a_o", (RLO, CH), BF16)
            ln1_sb = sb4("ln1_sb", (128, D), F32)
            a_ep = sb4("a_ep", (128, D), F32)
            ln3tmb = sb4("ln3tmb", (128, D), BF16)
            oA_sb = sb4("oA_sb", (128, NIC, RLO), BF16)
            oB_sb = sb4("oB_sb", (RLO, D), BF16)

            bar4 = barrier()
            wo_toks = [wcb(wo_full[:, :, d0 * 512:(d0 + 1) * 512],
                           di["woT"][:, d0 * 512:(d0 + 1) * 512]
                           .rearrange("(c p) m -> p c m", p=128),
                           waits=bar4 if d0 == 0 else (), sem=s_wd[d0])
                       for d0 in range(4)]
            tok_wo = wo_toks[-1]
            tok_oA = wcb(oA_sb[:], rr(di["oAT"]))
            tok_oB = wcb(oB_sb[:], di["oBT"])
            ao_toks = []
            for n0 in range(0, CH, 512):
                for ic in range(NIC):
                    mm(lambda e, j=ic, nn=n0, s=(ic == 0), q=(ic == NIC - 1):
                       e.matmul(PB[7][0:16, :], oA_sb[:, j, :],
                                O_fm[:, j, nn:nn + 512], start=s, stop=q),
                       waits=([tok_oA] + bar4 + o_evac_toks) if ic == 0 else [],
                       wbanks=(7,) if ic == 0 else (),
                       inc=sPE if ic == NIC - 1 else None,
                       stop_banks=(7,) if ic == NIC - 1 else None)
                ao_toks.append(rd("vector", lambda e, nn=n0: e.tensor_scalar(
                    a_o[:, nn:nn + 512], PB[7][0:16, :], LORA_SCALE, None, ALU.mult),
                    inc=sDV, banks=(7,)))

            x2_toks = []
            l3w_toks = []
            l3ev_toks = []
            l3tl_toks = []

            def emit_l3T(qt2):
                tb_banks = (0, 1, 2, 3) if qt2 % 2 == 0 else (4, 5, 6, 7)
                tmmT = None
                for ic in range(NIC):
                    bk = tb_banks[ic % 4]
                    tmmT = mm(lambda e, j=ic, q=PB[bk]: e.matmul(
                        q[:, 0:128], ln3tmb[:, j * 128:(j + 1) * 128], ident_b[:],
                        start=True, stop=True),
                        waits=[l3tl_toks[qt2], tok_ident], wbanks=(bk,), inc=sPE,
                        stop_banks=(bk,))
                    l3ev_toks.append(rd("vector", lambda e, j=ic, q=PB[bk],
                                        qq=qt2: e.tensor_copy(
                                            ln3_fm[:, j, qq * 128:(qq + 1) * 128],
                                            q[:, 0:128]),
                                        inc=sDV, banks=(bk,)))
                l3w_toks.append(tmmT)
            ep_xt_guard = [list(bar4), list(bar4)]
            last_ep = [None, None]
            last_opro = [None]
            prev_tx2 = [None, None]
            for qt in range(NQT):
                bs = 0 if qt % 2 == 0 else 4
                op_toks = []
                for dc in range(4):
                    bk = bs + dc
                    pb = PB[bk]
                    last_opro[0] = None
                    for ic in range(NIC):
                        mm(lambda e, j=ic, q=qt, d=dc, p=pb: e.matmul(
                            p[:], O_fm[:, j, q * 128:(q + 1) * 128],
                            wo_full[:, j, d * 512:(d + 1) * 512],
                            start=(j == 0), stop=False),
                           waits=([wo_toks[dc], tok_oB] + ao_toks)
                           if ic == 0 else [],
                           wbanks=(bk,) if ic == 0 else ())
                    op_toks.append(mm(lambda e, q=qt, d=dc, p=pb: e.matmul(
                        p[:], a_o[0:16, q * 128:(q + 1) * 128],
                        oB_sb[0:16, d * 512:(d + 1) * 512], start=False, stop=True),
                        inc=sPE, stop_banks=(bk,)))
                    last_opro[0] = op_toks[-1]
                if qt >= 1:
                    emit_l3T(qt - 1)
                xb = xt[qt % 2]
                tok_x = ld(xb[:], di["xs"][r0 + HALO + qt * 128:
                                           r0 + HALO + (qt + 1) * 128, :],
                           waits=bar4 + ep_xt_guard[qt % 2], sem=s_x[qt % 2])
                tb1 = E("vector", lambda e, q=qt: e.scalar_tensor_tensor(
                    col(cC, 8), col(m1c, q), -1.0, col(rs1c, q), ALU.mult, ALU.mult),
                    inc=sDV, tiny=True)
                ln1_guard = [x2_toks[qt - 1]] if qt >= 1 else []
                tln1 = E("scalar", lambda e, q=qt, x=xb: e.activation(
                    ln1_sb[:], x[:], AF.Identity, bias=col(cC, 8),
                    scale=col(rs1c, q)),
                    waits=[tok_x, tb1] + ln1_guard, inc=sAC)
                ta = []
                for dc in range(4):
                    bk = bs + dc
                    ta.append(rd("vector", lambda e, d=dc, p=PB[bs + dc]:
                                 e.scalar_tensor_tensor(
                                     a_ep[:, d * 512:(d + 1) * 512], p[:], 1.0,
                                     ln1_sb[:, d * 512:(d + 1) * 512], ALU.mult,
                                     ALU.add, accum_out=col(cA, d)),
                                 waits=[tln1, op_toks[dc]], inc=sDV, banks=(bk,),
                                 tiny=True))
                tcm1 = E("vector", lambda e: e.tensor_tensor(
                    col(cA, 4), col(cA, 0), col(cA, 1), op=ALU.add),
                    waits=ta, inc=sDV, tiny=True)
                E("vector", lambda e: e.tensor_tensor(col(cA, 5), col(cA, 2),
                                                      col(cA, 3), op=ALU.add),
                  inc=sDV, tiny=True)
                E("vector", lambda e: e.tensor_tensor(col(cA, 0), col(cA, 4),
                                                      col(cA, 5), op=ALU.add),
                  inc=sDV, tiny=True)
                tsqa = E("scalar", lambda e: e.activation(
                    sq[:], a_ep[:], AF.Square, accum_out=col(cA, 1)),
                    waits=[tcm1], inc=sAC, tiny=True)
                E("vector", lambda e: e.tensor_scalar(cB[:, 0:2], cA[:, 0:2],
                                                      1.0 / D, None, ALU.mult),
                  waits=[tsqa], inc=sDV, tiny=True)
                E("vector", lambda e: e.tensor_tensor(col(cC, 0), col(cB, 0),
                                                      col(cB, 0), op=ALU.mult),
                  inc=sDV, tiny=True)
                tv2 = E("vector", lambda e: e.tensor_tensor(
                    col(cC, 1), col(cB, 1), col(cC, 0), op=ALU.subtract), inc=sDV,
                    tiny=True)
                E("scalar", lambda e: e.activation(col(cD, 0), col(cC, 1), AF.Ln,
                                                   bias=EPS_LN), waits=[tv2],
                  inc=sAC, tiny=True)
                trs2 = E("scalar", lambda e: e.activation(col(cD, 1), col(cD, 0),
                                                          AF.Exp, scale=-0.5),
                         inc=sAC, tiny=True)
                tx2a = E("vector", lambda e: e.tensor_scalar(
                    sq[:], a_ep[:], col(cB, 0), col(cD, 1), ALU.subtract, ALU.mult),
                    waits=[trs2], inc=sDV)
                tx2 = E("vector", lambda e, x=xb: e.tensor_tensor(
                    ln1_sb[:], sq[:], x[:], op=ALU.add), inc=sDV)
                ep_xt_guard[qt % 2] = [tx2]
                last_ep[0] = tx2
                x2_toks.append(stb(x2d[qt * 128:(qt + 1) * 128, :],
                                   ln1_sb[:], waits=[tx2], sem=s_x2d))
                tsq3 = E("vector", lambda e: e.scalar_tensor_tensor(
                    sq[:], ln1_sb[:], 1.0, ln1_sb[:], ALU.mult, ALU.mult,
                    accum_out=col(cA, 7)), waits=[tx2], inc=sDV, tiny=True)
                E("vector", lambda e: e.tensor_scalar(col(cB, 7), col(cA, 7),
                                                      1.0 / D, None, ALU.mult),
                  inc=sDV, tiny=True)
                E("vector", lambda e, q=qt: e.tensor_tensor(
                    col(cC, 0), col(m1c, q), col(m1c, q), op=ALU.mult),
                  inc=sDV, tiny=True)
                tv3 = E("vector", lambda e: e.tensor_tensor(
                    col(cC, 1), col(cB, 7), col(cC, 0), op=ALU.subtract), inc=sDV,
                    tiny=True)
                E("scalar", lambda e: e.activation(col(cD, 2), col(cC, 1), AF.Ln,
                                                   bias=EPS_LN), waits=[tv3],
                  inc=sAC, tiny=True)
                trs3 = E("scalar", lambda e: e.activation(col(cD, 3), col(cD, 2),
                                                          AF.Exp, scale=-0.5),
                         inc=sAC, tiny=True)
                tb3 = E("vector", lambda e, q=qt: e.scalar_tensor_tensor(
                    col(cC, 3), col(m1c, q), -1.0, col(cD, 3), ALU.mult,
                    ALU.mult), waits=[trs3], inc=sDV, tiny=True)
                tc7 = E("vector", lambda e: e.tensor_copy(col(cC, 7), col(cD, 3)),
                        inc=sDV, tiny=True)
                l3_guard = [l3w_toks[qt - 1]] if qt >= 1 else []
                tl3 = E("scalar", lambda e: e.activation(
                    ln3tmb[:], ln1_sb[:], AF.Identity, bias=col(cC, 3),
                    scale=col(cC, 7)), waits=[tb3, tc7] + l3_guard, inc=sAC)
                l3tl_toks.append(tl3)
                last_ep[1] = tl3
            emit_l3T(NQT - 1)
            l3_toks = l3ev_toks
            if chunk == 0 and tap("ln3") is not None:
                stb(tap("ln3"), ln3_fm[:], waits=l3_toks, sem=s_tap)
            if chunk == 0 and tap("x2") is not None:
                stb(tap("x2")[0:CH, :], x2d, waits=x2_toks, sem=s_tap)
            a1_next = None
            if chunk + 1 < NCH:
                a1_next = emit_A1(chunk + 1,
                                  pre=[last_ep[0], last_ep[1], a2_tr_val])

            actx4.close()
            actxO.close()
            actxB = contextlib.ExitStack()
            sbB = lambda n, s, d: actxB.enter_context(nc.sbuf_tensor(f"{n}_c{chunk}", list(s), d))
            wg_c = [sbB(f"wg_c{i}", (128, NIC, FFC), BF16) for i in range(2)]
            wu_c = [sbB(f"wu_c{i}", (128, NIC, FFC), BF16) for i in range(2)]
            bg_c = [sbB(f"bg_c{i}", (128, NE, FFC), BF16) for i in range(2)]
            bu_c = [sbB(f"bu_c{i}", (128, NE, FFC), BF16) for i in range(2)]
            ad_c = [sbB(f"ad_c{i}", (128, 2, NE, 128), BF16) for i in range(2)]
            ags_sb = sbB("ags_sb", (128, NIC, 128), BF16)
            aus_sb = sbB("aus_sb", (128, NIC, 128), BF16)
            g_sb = sbB("g_sb", (128, 2, CH), BF16)
            u_sb = sbB("u_sb", (128, 2, CH), BF16)
            silu_sb = sbB("silu_sb", (128, CH), F32)
            he_sb = sbB("he_sb", (128, CH), BF16)
            hbt = [sbB(f"hbt{i}", (128, CH), BF16) for i in range(2)]
            a_g3 = [sbB(f"a_g3_{g}", (128, CH), BF16) for g in range(NG)]
            a_u3 = [sbB(f"a_u3_{g}", (128, CH), BF16) for g in range(NG)]

            barB = barrier()
            c3z = [E("vector", lambda e, g2=g2: e.memset(c3[g2][:], 0.0),
                     waits=barB, inc=sDV) for g2 in range(NG)]
            mz_bg = [E("vector", lambda e, t=bg_c[i]: e.memset(t[:], 0.0),
                       waits=barB, inc=sDV) for i in range(2)]
            mz_bu = [E("vector", lambda e, t=bu_c[i]: e.memset(t[:], 0.0),
                       waits=barB, inc=sDV) for i in range(2)]

            werm_toks = []
            for qt in range(NQT):
                for ic in range(NIC):
                    mm(lambda e, j=ic, q=qt, s=(ic == 0), z=(ic == NIC - 1):
                       e.matmul(PB[7][:, 0:NE], ln3_fm[:, j, q * 128:(q + 1) * 128],
                                wr_sb[:, j, :], start=s, stop=z),
                       waits=([tok_wr] + l3_toks + barB) if ic == 0 else [],
                       wbanks=(7,) if ic == 0 else (),
                       inc=sPE if ic == NIC - 1 else None,
                       stop_banks=(7,) if ic == NIC - 1 else None)
                rd("vector", lambda e: e.tensor_copy(rt_a[:], PB[7][:, 0:NE]),
                   inc=sDV, banks=(7,), tiny=True)
                tmx = E("vector", lambda e: e.tensor_reduce(
                    col(cA, 9), rt_a[:], axis=AX.X, op=ALU.max, negate=True),
                    inc=sDV, tiny=True)
                tex = E("scalar", lambda e: e.activation(
                    rt_b[:], rt_a[:], AF.Exp, bias=col(cA, 9),
                    accum_out=col(cA, 10)), waits=[tmx], inc=sAC, tiny=True)
                E("scalar", lambda e: e.activation(col(cD, 9), col(cA, 10), AF.Ln),
                  inc=sAC, tiny=True)
                tiv = E("scalar", lambda e: e.activation(col(cD, 10), col(cD, 9),
                                                         AF.Exp, scale=-1.0),
                        inc=sAC, tiny=True)
                tk1 = E("vector", lambda e: e.tensor_scalar(
                    rt_a[:], rt_b[:], 1.0, None, ALU.is_ge), waits=[tex], inc=sDV,
                    tiny=True)
                E("vector", lambda e: e.tensor_tensor(rt_a[:], rt_b[:], rt_a[:],
                                                      op=ALU.mult), inc=sDV,
                  tiny=True)
                E("vector", lambda e: e.tensor_tensor(rt_a[:], rt_b[:], rt_a[:],
                                                      op=ALU.subtract), inc=sDV,
                  tiny=True)
                E("vector", lambda e: e.tensor_reduce(
                    col(cA, 11), rt_a[:], axis=AX.X, op=ALU.max), inc=sDV,
                  tiny=True)
                E("vector", lambda e: e.tensor_copy(col(cA, 12), col(cA, 11)),
                  inc=sDV, tiny=True)
                tkp = E("vector", lambda e: e.tensor_scalar(
                    rt_a[:], rt_b[:], col(cA, 11), None, ALU.is_ge), inc=sDV,
                    tiny=True)
                E("vector", lambda e: e.tensor_tensor(rt_b[:], rt_b[:], rt_a[:],
                                                      op=ALU.mult), inc=sDV,
                  tiny=True)
                twe = E("vector", lambda e: e.tensor_scalar(
                    rt_b[:], rt_b[:], col(cD, 10), None, ALU.mult),
                    waits=[tiv], inc=sDV, tiny=True)
                twt = mm(lambda e: e.matmul(PB[6][0:NE, 0:128], rt_b[:],
                                            ident_f[:], start=True, stop=True),
                         waits=[twe, tok_identf], wbanks=(6,), inc=sPE,
                         stop_banks=(6,))
                werm_toks.append(rd("vector", lambda e, q=qt: e.tensor_copy(
                    we_rm[:, q * 128:(q + 1) * 128], PB[6][0:NE, 0:128]),
                    inc=sDV, banks=(6,), tiny=True))

            def avec3(dstl, Asb, srcs):
                out = []
                prev_mm = None
                for g2 in range(NG):
                    tok_w = wcb(Asb[:], rr(srcs[g2]),
                                waits=[prev_mm] if prev_mm else barB)
                    for i, n0 in enumerate(range(0, CH, 512)):
                        bk = i % 2
                        pb = PB[bk]
                        for ic in range(NIC):
                            prev_mm = mm(lambda e, j=ic, p=pb, nn=n0,
                                         s=(ic == 0), z=(ic == NIC - 1):
                                         e.matmul(p[:], Asb[:, j, :],
                                                  ln3_fm[:, j, nn:nn + 512],
                                                  start=s, stop=z),
                                         waits=([tok_w] + l3_toks)
                                         if ic == 0 else [],
                                         wbanks=(bk,) if ic == 0 else (),
                                         inc=sPE if ic == NIC - 1 else None,
                                         stop_banks=(bk,)
                                         if ic == NIC - 1 else None)
                        out.append(rd("vector", lambda e, p=pb, nn=n0, g3=g2:
                                      e.tensor_scalar(dstl[g3][:, nn:nn + 512],
                                                      p[:], SCALING, None,
                                                      ALU.mult),
                                      inc=sDV, banks=(bk,)))
                return out
            ag_toks = avec3(a_g3, ags_sb, [di["AgS"][g] for g in range(NG)])
            au_toks = avec3(a_u3, aus_sb, [di["AuS"][g] for g in range(NG)])

            wB = [None, None]
            def ffw(fi, waits=()):
                p = fi % 2
                f0 = fi * FFC
                wB[p] = [
                    wcb(wg_c[p][:], di["WgT"][:, f0:f0 + FFC].rearrange(
                        "(c p) m -> p c m", p=128), waits=waits, sem=s_w[p]),
                    wcb(wu_c[p][:], di["WuT"][:, f0:f0 + FFC].rearrange(
                        "(c p) m -> p c m", p=128), sem=s_w[p]),
                    wcb(ad_c[p][:, 0, :, :], di["Ad8"][:, f0:f0 + 128, :]
                        .rearrange("e p m -> p e m"), sem=s_w[p]),
                    wcb(ad_c[p][:, 1, :, :], di["Ad8"][:, f0 + 128:f0 + 256, :]
                        .rearrange("e p m -> p e m"), sem=s_w[p]),
                ]
                for g2x, exps in enumerate(GROUPS):
                    for jx, e_idx in enumerate(exps):
                        wB[p].append(wcb(
                            bg_c[p][32 * jx:32 * jx + 16, e_idx, :],
                            di["BgE"][e_idx, :, f0:f0 + FFC],
                            waits=[mz_bg[p]] if fi < 2 else [], sem=s_w[p]))
                        wB[p].append(wcb(
                            bu_c[p][32 * jx:32 * jx + 16, e_idx, :],
                            di["BuE"][e_idx, :, f0:f0 + FFC],
                            waits=[mz_bu[p]] if fi < 2 else [], sem=s_w[p]))
            ffw(0, waits=barB)
            hb_w_toks = []
            prev_he_pe = None
            prev_hbt = [None, None]
            for fi in range(NFFC):
                p = fi % 2
                wt = wB[p]
                if fi + 1 < NFFC:
                    ffw(fi + 1, waits=[prev_he_pe] if prev_he_pe else [])
                guev = [[None, None], [None, None]]
                hb_last = [None]
                for pt in range(2):
                    fc = 2 * fi + pt
                    for ic in range(NIC):
                        for nb, n0 in enumerate((0, 512)):
                            mm(lambda e, j=ic, q=PB[nb], nn=n0, w=wg_c[p], b=pt:
                               e.matmul(
                                q[:], w[:, j, b * 128:(b + 1) * 128],
                                ln3_fm[:, j, nn:nn + 512],
                                start=(j == 0), stop=(j == NIC - 1)),
                               waits=(wt + l3_toks) if ic == 0 else [],
                               wbanks=(nb,) if ic == 0 else (),
                               inc=sPE if ic == NIC - 1 else None,
                               stop_banks=(nb,) if ic == NIC - 1 else None)
                    for ic in range(NIC):
                        for nb, n0 in enumerate((0, 512)):
                            mm(lambda e, j=ic, q=PB[2 + nb], nn=n0, w=wu_c[p],
                               b=pt: e.matmul(
                                q[:], w[:, j, b * 128:(b + 1) * 128],
                                ln3_fm[:, j, nn:nn + 512],
                                start=(j == 0), stop=(j == NIC - 1)),
                               wbanks=(2 + nb,) if ic == 0 else (),
                               inc=sPE if ic == NIC - 1 else None,
                               stop_banks=(2 + nb,) if ic == NIC - 1 else None)
                    hb_done = []
                    for nb, n0 in enumerate((0, 512)):
                        tsl0 = rd("scalar", lambda e, q=PB[nb], nn=n0:
                                  e.activation(silu_sb[:, nn:nn + 512], q[:],
                                               AF.Silu),
                                  waits=[prev_he_pe], inc=sAC, banks=(nb,))
                        guev[pt][0] = rd(
                            "scalar", lambda e, q=PB[nb], nn=n0, b=pt:
                            e.activation(g_sb[:, b, nn:nn + 512], q[:],
                                         AF.Identity),
                            waits=[prev_he_pe], inc=sAC, banks=(nb,))
                        guev[pt][1] = rd(
                            "vector", lambda e, q=PB[2 + nb], nn=n0, b=pt:
                            e.tensor_copy(u_sb[:, b, nn:nn + 512], q[:]),
                            waits=[prev_he_pe], inc=sDV, banks=(2 + nb,))
                        hb_done.append(E("vector", lambda e, nn=n0, b=pt:
                                         e.tensor_tensor(
                                             hbt[b][:, nn:nn + 512],
                                             silu_sb[:, nn:nn + 512],
                                             u_sb[:, b, nn:nn + 512],
                                             op=ALU.mult),
                                         waits=[prev_hbt[pt], tsl0], inc=sDV))
                    tok_hw = stb(hbd[fc * 128:(fc + 1) * 128, :], hbt[pt][:],
                                 waits=hb_done, sem=s_hb)
                    prev_hbt[pt] = tok_hw
                    hb_w_toks.append(tok_hw)
                    hb_last[0] = hb_done[-1]
                slots = [(g2, j, e_id, pt, nb)
                         for g2, exps in enumerate(GROUPS)
                         for j, e_id in enumerate(exps)
                         for pt in range(2)
                         for nb in range(2)]
                ns = len(slots)
                pend = [None] * ns
                the_toks = [None] * ns
                ad_toks = [None] * ns
                glb = lambda i: 4 + 2 * (i % 2)
                ulb = lambda i: 5 + 2 * (i % 2)

                def emit_slot_mms(i):
                    g2, j, e_id, pt, nb = slots[i]
                    n0 = nb * 512
                    mm(lambda e, q=PB[glb(i)], nn=n0, b=pt: e.matmul(
                        q[:], ident_b[:], g_sb[:, b, nn:nn + 512],
                        start=True, stop=False),
                       waits=((wt + ag_toks + au_toks) if i == 0 else [])
                       + guev[pt], wbanks=(glb(i),))
                    bg_tok = mm(lambda e, q=PB[glb(i)], nn=n0, ee=e_id, g3=g2,
                                w=bg_c[p], b=pt: e.matmul(
                                    q[:], w[:, ee, b * 128:(b + 1) * 128],
                                    a_g3[g3][:, nn:nn + 512],
                                    start=False, stop=True),
                                inc=sPE, stop_banks=(glb(i),))
                    mm(lambda e, q=PB[ulb(i)], nn=n0, b=pt: e.matmul(
                        q[:], ident_b[:], u_sb[:, b, nn:nn + 512],
                        start=True, stop=False),
                       waits=guev[pt], wbanks=(ulb(i),))
                    bu_tok = mm(lambda e, q=PB[ulb(i)], nn=n0, ee=e_id, g3=g2,
                                w=bu_c[p], b=pt: e.matmul(
                                    q[:], w[:, ee, b * 128:(b + 1) * 128],
                                    a_u3[g3][:, nn:nn + 512],
                                    start=False, stop=True),
                                inc=sPE, stop_banks=(ulb(i),))
                    pend[i] = (bg_tok, bu_tok)

                def emit_slot_ew(i):
                    g2, j, e_id, pt, nb = slots[i]
                    n0 = nb * 512
                    tsl = rd("scalar", lambda e, q=PB[glb(i)], nn=n0:
                             e.activation(silu_sb[:, nn:nn + 512], q[:],
                                          AF.Silu),
                             waits=[pend[i][0]]
                             + ([hb_last[0]] if i < 2 else [the_toks[i - 2]]),
                             inc=sAC, banks=(glb(i),))
                    the_toks[i] = rd(
                        "vector", lambda e, q=PB[ulb(i)], nn=n0:
                        e.tensor_tensor(
                            he_sb[:, nn:nn + 512], silu_sb[:, nn:nn + 512],
                            q[:], op=ALU.mult),
                        waits=[tsl] + ([ad_toks[i - 2]] if i >= 2 else []),
                        inc=sDV, banks=(ulb(i),))

                def emit_slot_ad(i):
                    nonlocal prev_he_pe
                    g2, j, e_id, pt, nb = slots[i]
                    n0 = nb * 512
                    first = (j == 0 and pt == 0)
                    last = (j == len(GROUPS[g2]) - 1 and pt == 1)
                    ad_tok = mm(lambda e, q=PB[nb], nn=n0, ee=e_id, b=pt,
                                s=first, z=last, w=ad_c[p]: e.matmul(
                                    q[:], w[:, b, ee, :], he_sb[:, nn:nn + 512],
                                    start=s, stop=z),
                                waits=[the_toks[i]],
                                wbanks=(nb,) if first else (),
                                inc=sPE,
                                stop_banks=(nb,) if last else None)
                    ad_toks[i] = ad_tok
                    prev_he_pe = ad_tok
                    if last:
                        rd("vector", lambda e, q=PB[nb], nn=n0, g3=g2:
                           e.tensor_tensor(c3[g3][:, nn:nn + 512], q[:],
                                           c3[g3][:, nn:nn + 512], op=ALU.add),
                           waits=c3z, inc=sDV, banks=(nb,))

                emit_slot_mms(0)
                emit_slot_mms(1)
                emit_slot_ew(0)
                for i in range(2, ns):
                    emit_slot_mms(i)
                    emit_slot_ew(i - 1)
                    emit_slot_ad(i - 2)
                emit_slot_ew(ns - 1)
                emit_slot_ad(ns - 2)
                emit_slot_ad(ns - 1)
            if chunk == 0 and tap("c3") is not None:
                for g2 in range(NG):
                    stb(tap("c3")[:, g2, :], c3[g2][:], sem=s_tap)
            if chunk == 0 and tap("werm") is not None:
                stb(tap("werm"), we_rm[:], waits=werm_toks, sem=s_tap)

            actxB.close()
            ctxL.close()
            actxB2 = contextlib.ExitStack()
            sbB2 = lambda n, s, d: actxB2.enter_context(
                nc.sbuf_tensor(f"{n}_c{chunk}", list(s), d))
            wd_dc = sbB2("wd_dc", (128, NFT, 1024), BF16)
            h_t = [sbB2(f"h_t{i}", (128, NFT, 128), BF16) for i in range(2)]
            fst = [sbB2(f"fst{i}", (128, 1024), F32) for i in range(2)]
            c3b = [sbB2(f"c3b_{g}", (128, CH), BF16) for g in range(NG)]
            bds_sb = sbB2("bds_sb", (128, NG, 1024), BF16)
            fbB = [sbB2("fbB0", (128, D), F32), sbB2("fbB1", (128, D), F32)]
            sqB = sbB2("sqB", (128, D), F32)
            cB4 = sbB2("cB4", (128, 8), F32)

            barB2 = barrier()
            c3b_toks = []
            for g2 in range(NG):
                for nb, n0 in enumerate((0, 512)):
                    tw = mm(lambda e, q=PB[4 + nb], nn=n0, g3=g2: e.matmul(
                        q[:], rep_sb[:, g3, :], we_rm[0:NE, nn:nn + 512],
                        start=True, stop=True),
                        waits=[tok_rep] + werm_toks + barB2, wbanks=(4 + nb,),
                        inc=sPE, stop_banks=(4 + nb,))
                    c3b_toks.append(rd("vector", lambda e, q=PB[4 + nb], nn=n0,
                                       g3=g2: e.tensor_tensor(
                                           c3b[g3][:, nn:nn + 512],
                                           c3[g3][:, nn:nn + 512], q[:],
                                           op=ALU.mult), inc=sDV,
                                       banks=(4 + nb,)))
            fd_w = [[] for _ in range(NQT)]
            prev_ht_mm = [None, None]
            prev_fst = [None, None]
            def wd_load(dcp, fg, waits=()):
                f0, f1 = fg * 11, (fg + 1) * 11
                return wcb(wd_dc[:, f0:f1, :],
                           di["WdT"][f0 * 128:f1 * 128,
                                     dcp * 1024:(dcp + 1) * 1024]
                           .rearrange("(c p) m -> p c m", p=128),
                           waits=waits, sem=s_wd[fg])
            def bds_load(dcp, waits=()):
                return wcb(bds_sb[:],
                           di["BdS"][:, :, dcp * 1024:(dcp + 1) * 1024]
                           .rearrange("a b c -> b a c"), waits=waits)
            bds_toks = [bds_load(0, waits=barB2), None]
            wd_toks2 = [[wd_load(0, fg, waits=barB2) for fg in range(4)], None]
            fg_last = [None] * 4
            for dcp in range(2):
                wd_toks = wd_toks2[dcp]
                tok_bds = bds_toks[dcp]
                for t in range(NQT):
                    bp = t % 2
                    tok_h = ld(h_t[bp][:, 0:22, :],
                               hbd[0:22 * 128, t * 128:(t + 1) * 128]
                               .rearrange("(c p) m -> p c m", p=128),
                               waits=hb_w_toks + [prev_ht_mm[bp]], sem=s_h2[bp])
                    tok_h2 = E("gpsimd", lambda e, tt=t, b=bp: e.dma_start(
                        out=h_t[b][:, 22:NFT, :],
                        in_=hbd[22 * 128:NFT * 128, tt * 128:(tt + 1) * 128]
                        .rearrange("(c p) m -> p c m", p=128)),
                        waits=hb_w_toks + [prev_ht_mm[bp]], inc=s_h3[bp],
                        amt=16, is_dma=True)
                    dmm = None
                    for nb in range(2):
                        bk = 2 * bp + nb
                        dc = 2 * dcp + nb
                        for fc in range(NFT):
                            w = [tok_h] if fc == 0 else []
                            if fc == 22 and nb == 0:
                                w.append(tok_h2)
                            if fc % 11 == 0:
                                w.append(wd_toks[fc // 11])
                            lastrd = (dcp == 0 and t == NQT - 1 and nb == 1
                                      and fc % 11 == 10)
                            mtk = mm(lambda e, f=fc, q=PB[bk], ht=h_t[bp],
                                     b2=nb: e.matmul(
                                q[:], ht[:, f, :],
                                wd_dc[:, f, b2 * 512:(b2 + 1) * 512],
                                start=(f == 0), stop=False),
                               waits=w, wbanks=(bk,) if fc == 0 else (),
                               inc=sPE if lastrd else None)
                            if lastrd:
                                fg_last[fc // 11] = mtk
                        for g2 in range(NG):
                            dmm = mm(lambda e, g3=g2, q=PB[bk], tt=t, b2=nb,
                                     dd=dcp: e.matmul(
                                q[:], c3b[g3][:, tt * 128:(tt + 1) * 128],
                                bds_sb[:, g3, b2 * 512:(b2 + 1) * 512],
                                start=False, stop=(g3 == NG - 1)),
                                waits=[c3b_toks[-1], tok_bds]
                                if g2 == 0 else [],
                                inc=sPE if g2 == NG - 1 else None,
                                stop_banks=(bk,) if g2 == NG - 1 else None)
                    prev_ht_mm[bp] = dmm
                    if dcp == 0 and t == NQT - 1:
                        wd_toks2[1] = [wd_load(1, fg, waits=[fg_last[fg]])
                                       for fg in range(4)]
                        bds_toks[1] = bds_load(1, waits=[dmm])
                    for nb in range(2):
                        bk = 2 * bp + nb
                        dc = 2 * dcp + nb
                        tev = rd("vector", lambda e, q=PB[bk], b=bp, b2=nb:
                                 e.tensor_copy(
                                     fst[b][:, b2 * 512:(b2 + 1) * 512], q[:]),
                                 waits=[prev_fst[bp]], inc=sDV, banks=(bk,))
                        prev_fst[bp] = tev
                    tfd = stb(fd[t * 128:(t + 1) * 128,
                                 dcp * 1024:(dcp + 1) * 1024],
                              fst[bp][:], waits=[prev_fst[bp]], sem=s_fd)
                    prev_fst[bp] = tfd
                    fd_w[t].append(tfd)
            b4_guard = [[], []]
            prev_tout = None
            for t in range(NQT):
                pb4 = t % 2
                fb = fbB[pb4]
                x2b = xt[pb4]
                xg = (a1_next["xt_guard"][pb4] if a1_next is not None
                      else ep_xt_guard[pb4])
                tok_f = ld(fb[:], fd[t * 128:(t + 1) * 128, :],
                           waits=fd_w[t] + b4_guard[pb4], sem=s_fd)
                tok_x2 = ld(x2b[:], x2d[t * 128:(t + 1) * 128, :],
                            waits=x2_toks + list(xg) + b4_guard[pb4], sem=s_x2d)
                E("scalar", lambda e, f=fb: e.activation(sqB[:], f[:], AF.Identity,
                                                         accum_out=col(cB4, 0)),
                  waits=[tok_f, tok_x2]
                  + ([prev_tout] if prev_tout else []), inc=sAC, tiny=True)
                tsq4 = E("scalar", lambda e, f=fb: e.activation(
                    sqB[:], f[:], AF.Square, accum_out=col(cB4, 1)), inc=sAC,
                    tiny=True)
                E("vector", lambda e: e.tensor_scalar(cB4[:, 2:4], cB4[:, 0:2],
                                                      1.0 / D, None, ALU.mult),
                  waits=[tsq4], inc=sDV, tiny=True)
                E("vector", lambda e: e.tensor_tensor(col(cB4, 4), col(cB4, 2),
                                                      col(cB4, 2), op=ALU.mult),
                  inc=sDV, tiny=True)
                tv4 = E("vector", lambda e: e.tensor_tensor(
                    col(cB4, 5), col(cB4, 3), col(cB4, 4), op=ALU.subtract),
                    inc=sDV, tiny=True)
                E("scalar", lambda e: e.activation(col(cB4, 6), col(cB4, 5),
                                                   AF.Ln, bias=EPS_LN),
                  waits=[tv4], inc=sAC, tiny=True)
                trs4 = E("scalar", lambda e: e.activation(col(cB4, 7), col(cB4, 6),
                                                          AF.Exp, scale=-0.5),
                         inc=sAC, tiny=True)
                tf1 = E("vector", lambda e, f=fb: e.tensor_scalar(
                    sqB[:], f[:], col(cB4, 2), col(cB4, 7), ALU.subtract,
                    ALU.mult), waits=[trs4], inc=sDV)
                tout = E("vector", lambda e, f=fb, x2_=x2b: e.tensor_tensor(
                    f[:], sqB[:], x2_[:], op=ALU.add), inc=sDV)
                prev_tout = tout
                ty = stb(y[r0 + t * 128: r0 + (t + 1) * 128, :], fb[:],
                         waits=[tout], sem=sST)
                b4_guard[pb4] = [ty]
            actxB2.close()
            ctxC.close()
            return a1_next, last_opro[0]
        a1 = emit_A1(0, [])
        prev_opro = None
        for chunk in range(NCH):
            a1, prev_opro = emit_chunk(chunk, a1, prev_opro)

        E("sync", lambda e: e.nop(), waits=[(sST, sST.n)])

        blk = ctx.enter_context(nc.Block())
        for s in P.sems:
            s.h = ctx.enter_context(nc.semaphore(s.name))

        def runner(my):
            def run(eng):
                ops = P.ops if max_ops is None else P.ops[:max_ops]
                for op_eng, fn, waits, inc, amt, tiny in ops:
                    if op_eng != my:
                        continue
                    for s, v in waits:
                        eng.wait_ge(s.h, v)
                    inst = fn(eng)
                    if tiny:
                        inst = eng.drain()
                    if inc is not None:
                        inst.then_inc(inc.h, amt)
            return run

        blk.tensor(runner("tensor"))
        blk.vector(runner("vector"))
        blk.scalar(runner("scalar"))
        blk.gpsimd(runner("gpsimd"))
        blk.sync(runner("sync"))

    global N_OPS_LAST
    N_OPS_LAST = len(P.ops)
    return nc, tap_aps


def kernel(**inputs):
    nc, _ = build_program()
    in_maps = [_prep_core_inputs(c, inputs) for c in range(NCORES)]
    res = run_bass_kernel_spmd(nc, in_maps, core_ids=list(range(NCORES)))
    out = np.empty((B, L, D), np.float32)
    for c in range(NCORES):
        b, half = c // 2, c % 2
        out[b, half * TC:(half + 1) * TC] = res.results[c]["y"]
    return out
